# version 10
# baseline (speedup 1.0000x reference)
import numpy as np
from contextlib import ExitStack

import concourse.bass as bass
import concourse.mybir as mybir
from concourse.alu_op_type import AluOpType as ALU
from concourse.bass_utils import run_bass_kernel_spmd

F32 = mybir.dt.float32
BF16 = mybir.dt.bfloat16
AF = mybir.ActivationFunctionType
AX = mybir.AxisListType.X

B, N, C = 4, 4096, 2048
H, HKV, D = 16, 8, 128
HQ, KV = 8, 4
TC = 2048
SCALE = float(1.0 / np.sqrt(128.0))
NSL, SL = 8, 512
CK = 16
CH, NCH = 128, 32
FLA_W = 0.01

ENGINES = ("sync", "gpsimd", "tensor", "vector", "scalar")


class Sem:
    def __init__(self, h):
        self.h = h
        self.n = 0


class P:

    def __init__(self, nc, ctx):
        self.nc = nc
        self.ctx = ctx
        self.streams = {e: [] for e in ENGINES}
        self._sems = {}
        self.self_sems = {}

    def sem(self, name):
        if name not in self._sems:
            self._sems[name] = Sem(self.ctx.enter_context(self.nc.semaphore(name)))
        return self._sems[name]

    def op(self, engine, f, inc=None, inck=1, raw=False):
        if not raw and engine in ("vector", "scalar"):
            own = self.self_sems.get(engine)
            if own is None:
                own = self.sem("self_" + engine)
                self.self_sems[engine] = own
            if inc is None:
                inc = own
            assert inc is own, "vector/scalar ops must inc their own sem"
            self.wait(engine, own, own.n)
        tok = None
        h = None
        if inc is not None:
            inc.n += inck
            tok = inc.n
            h = inc.h

        def run(eng, f=f, h=h, inck=inck):
            ins = f(eng)
            if h is not None:
                ins.then_inc(h, inck)
        self.streams[engine].append(run)
        return tok

    def wait(self, engine, sem, val):
        if val is None or val <= 0:
            return
        self.streams[engine].append(lambda eng, s=sem.h, v=val: eng.wait_ge(s, v))

    def wait_all(self, engine, sem):
        self.wait(engine, sem, sem.n)

    def dma(self, engine, out, in_, sem, **kw):
        return self.op(engine,
                       lambda e, o=out, i=in_, kw=kw: e.dma_start(out=o, in_=i, **kw),
                       inc=sem, inck=16, raw=True)

    def barrier(self, name):
        s = self.sem("bar_" + name)
        for e in ENGINES:
            self.op(e, lambda eng, s=s: eng.sem_inc(s.h, 1), raw=True)
            s.n += 1
        v = s.n
        for e in ENGINES:
            self.wait(e, s, v)

    def emit(self, block):
        for e in ENGINES:
            fns = self.streams[e]
            getattr(block, e)(lambda eng, fns=fns: [f(eng) for f in fns])



def stage(inp):
    import ml_dtypes
    c = np.ascontiguousarray
    x = np.asarray(inp["L0_attn_a_hidden_states"])
    wq = np.asarray(inp["L0_attn_a_Wq"])
    wk = np.asarray(inp["L0_attn_a_Wk"])
    wv = np.asarray(inp["L0_attn_a_Wv"])

    shared = {
        "wo": c(np.asarray(inp["L0_attn_a_Wo"])),
        "wg": c(np.asarray(inp["L0_attn_a_Wg"])),
        "bg": c(np.asarray(inp["L0_attn_a_bg"]).reshape(1, 1)),
        "wqkvT": c(np.asarray(inp["L0_attn_b_w_qkv"]).T),
        "wprojT": c(np.asarray(inp["L0_attn_b_w_proj"]).T),
        "bproj": c(np.asarray(inp["L0_attn_b_b_proj"]).reshape(C, 1)),
        "gwT": c(np.asarray(inp["L0_ffn_gate_w"]).T),
        "wguI": None,
        "shguI": None,
        "wdT": c(np.asarray(inp["L0_ffn_w_down"]).transpose(0, 2, 1)),
        "shdT": c(np.asarray(inp["L0_ffn_sh_down"]).T),
        "segT": c(np.asarray(inp["L0_ffn_se_gate_w"]).T),
        "segR": c(np.asarray(inp["L0_ffn_se_gate_w"]).reshape(16, 128).T),
        "bprojR": c(np.asarray(inp["L0_attn_b_b_proj"]).reshape(16, 128).T),
    }
    wgT = np.asarray(inp["L0_ffn_w_gate"]).transpose(0, 2, 1)
    wuT = np.asarray(inp["L0_ffn_w_up"]).transpose(0, 2, 1)
    shared["wguI"] = c(np.stack([wgT.reshape(8, C, 8, 128),
                                 wuT.reshape(8, C, 8, 128)], axis=3)
                       .reshape(8, C, 2048))
    sgT = np.asarray(inp["L0_ffn_sh_gate"]).T.reshape(C, 16, 128)
    suT = np.asarray(inp["L0_ffn_sh_up"]).T.reshape(C, 16, 128)
    shared["shguI"] = c(np.stack([sgT, suT], axis=2).reshape(C, 4096))
    ident = np.eye(128, dtype=np.float32)
    shared["ident_f"] = ident
    shared["ident_b"] = ident.astype(ml_dtypes.bfloat16)
    tl = np.tril(np.ones((128, 128), np.float32), -1)
    shared["mask_ls"] = tl
    shared["mask_us"] = c(tl.T)
    shared["mask_ui"] = c(np.triu(np.ones((128, 128), np.float32), 0))
    ones = np.ones((128, 512), np.float32)
    shared["ones_f"] = ones
    shared["ones_c"] = np.ones((128, 1), np.float32)
    shared["ones_b"] = ones.astype(ml_dtypes.bfloat16)
    cm = np.zeros((4, 128, 512), np.float32)
    kk, qq = np.meshgrid(np.arange(128), np.arange(512), indexing="ij")
    for s_ in range(4):
        cm[s_] = (128 * s_ + kk <= qq).astype(np.float32)
    shared["cmask"] = c(cm)
    ish = np.zeros((128, 1024), np.float32)
    ish[np.arange(128), 384 + np.arange(128)] = 1.0
    shared["ishift"] = ish

    in_maps = []
    xt = [c(x[b_].T) for b_ in range(B)]
    for core in range(8):
        b_, par = core // 2, core % 2
        m = dict(shared)
        m["x_fm"] = xt[b_]
        m["x_own"] = c(xt[b_][:, par * TC:(par + 1) * TC])
        m["wq_p"] = c(wq[:, par * 1024:(par + 1) * 1024])
        m["wk_p"] = c(wk[:, par * 512:(par + 1) * 512])
        m["wv_p"] = c(wv[:, par * 512:(par + 1) * 512])
        m["wkT_p"] = c(wk.T[par * 512:(par + 1) * 512, :])
        ps_ = np.zeros((128, 2), np.float32)
        ps_[:, par] = 1.0
        m["psel"] = ps_
        in_maps.append(m)
    return in_maps



class K:

    def __init__(self, nc, ctx, dbg):
        self.nc = nc
        self.ctx = ctx
        self.dbg = set(dbg)
        self.p = P(nc, ctx)

        din = lambda n, s, d=F32: nc.dram_tensor(n, s, d, kind="ExternalInput").ap()
        self.x_fm = din("x_fm", [C, N])
        self.x_own = din("x_own", [C, TC])
        self.wq_p = din("wq_p", [C, 1024])
        self.wk_p = din("wk_p", [C, 512])
        self.wv_p = din("wv_p", [C, 512])
        self.wkT_p = din("wkT_p", [512, C])
        self.wo = din("wo", [C, C])
        self.wg = din("wg", [128, 1])
        self.bg = din("bg", [1, 1])
        self.wqkvT = din("wqkvT", [C, 3 * C])
        self.wprojT = din("wprojT", [C, C])
        self.bproj = din("bproj", [C, 1])
        self.gwT = din("gwT", [C, 8])
        self.wguI = din("wguI", [8, C, 2048])
        self.wdT = din("wdT", [8, 1024, C])
        self.shguI = din("shguI", [C, 4096])
        self.shdT = din("shdT", [C, C])
        self.segT = din("segT", [C, 1])
        self.segR = din("segR", [128, 16])
        self.bprojR = din("bprojR", [128, 16])
        self.ident_f = din("ident_f", [128, 128])
        self.ident_b = din("ident_b", [128, 128], BF16)
        self.mask_ls = din("mask_ls", [128, 128])
        self.mask_us = din("mask_us", [128, 128])
        self.mask_ui = din("mask_ui", [128, 128])
        self.ones_f = din("ones_f", [128, 512])
        self.ones_c = din("ones_c", [128, 1])
        self.ones_b = din("ones_b", [128, 512], BF16)
        self.cmask = din("cmask", [4, 128, 512])
        self.ishift = din("ishift", [128, 1024])
        self.psel = din("psel", [128, 2])

        def scr(n, s, d):
            kind = "ExternalOutput" if n in self.dbg else "Internal"
            return nc.dram_tensor(n, s, d, kind=kind).ap()
        self.q_fm = scr("q_fm", [1024, N], BF16)
        self.qh_fm = scr("qh_fm", [1024, N], BF16)
        self.k_fm = scr("k_fm", [512, N], BF16)
        self.kh_fm = scr("kh_fm", [512, N], BF16)
        self.kh_tm = scr("kh_tm", [KV, N, 128], BF16)
        self.v_tm = scr("v_tm", [N, 512], BF16)
        self.gs_d = scr("gs_d", [KV, N], F32)
        self.contrib = scr("contrib", [2 * C, TC // 2], F32)
        self.gath = scr("gath", [4 * C, TC // 2], F32)
        self.x1r = scr("x1r", [C, TC], F32)
        self.x2r = scr("x2r", [C, TC], F32)
        self.x3r = scr("x3r", [C, TC], F32)
        self.ln_d = scr("ln_d", [C, TC], BF16)
        self.vkp = scr("vkp", [33, C], F32)
        self.vkr = scr("vkr", [33, C], F32)
        self.qb_fm = scr("qb_fm", [C, TC], BF16)
        self.kb_tm = scr("kb_tm", [TC, C], BF16)
        self.vb_tm = scr("vb_tm", [TC, C], BF16)
        self.abo = scr("abo", [C, TC], BF16)
        self.out_fm = nc.dram_tensor("out_fm", [C, TC], F32, kind="ExternalOutput").ap()
        self.wguC = scr("wguC", [8, C, 2048], BF16)
        self.wdC = scr("wdC", [8, 1024, C], BF16)
        self.shguC = scr("shguC", [C, 4096], BF16)
        self.shdC = scr("shdC", [C, C], BF16)

        s = lambda n, sh, d: ctx.enter_context(nc.sbuf_tensor(n, sh, d))
        self.xs = s("xs", [128, CK * SL], F32)
        self.lns = s("lns", [128, CK * SL], BF16)
        self.big0 = s("big0", [128, 32768], BF16)
        self.big1 = s("big1", [128, 32768], BF16)
        self.wq_sb = self.big0[:, 0:16384]
        self.wk_sb = self.big0[:, 16384:24576]
        self.wv_sb = self.big0[:, 24576:32768]
        self.wg2 = s("wg2", [128, 64], F32)
        self.wg2b = s("wg2b", [128, 64], BF16)
        self.wgv = s("wgv", [128, 1], F32)
        self.bgv = s("bgv", [1, 1], F32)
        self.wkT_sb = self.xs[:, 0:2048]
        self.eq = s("eq", [128, 512], F32)
        self.sq = [s(f"sq{i}", [128, 512], F32) for i in range(2)]
        self.rowa = s("rowa", [1, 512], F32)
        self.rowb = s("rowb", [1, 512], F32)
        self.rowc = s("rowc", [1, 512], F32)
        self.gsst = s("gsst", [1, 512], F32)
        self.ta = s("ta", [128, 512], BF16)
        self.tb = s("tb", [128, 512], BF16)
        self.onescol_f = s("onescol_f", [128, 1], F32)
        self.cstp = s("cstp", [128, 16], F32)
        self._cst = {}
        self.onesrow_f = s("onesrow_f", [1, 512], F32)
        self.identb = s("identb", [128, 128], BF16)
        self.identf = s("identf", [128, 128], F32)

        self.ps = [ctx.enter_context(nc.psum_tensor(f"ps{i}", [128, 512], F32))
                   for i in range(8)]

        p = self.p
        self.qld = p.sem("qld")
        self.cini = p.sem("cini")
        self.res_gsst = 0
        for v_ in (1e-5, -0.5, -1.0, 1.0 / 16.0, 2.0, float(SCALE * FLA_W),
                   0.99, 0.01, float(SCALE), 0.5, -2.0):
            self.constf(v_)
        self.qst = p.sem("qst")
        self.qgp = p.sem("qgp")
        self.pe = p.sem("pe")
        self.dve = p.sem("self_vector")
        self.act = p.sem("self_scalar")
        p.self_sems["vector"] = self.dve
        p.self_sems["scalar"] = self.act

    def constf(self, val):
        val = float(val)
        if val not in self._cst:
            i = len(self._cst)
            assert i < 16
            self._cst[val] = i
            self.nc.const_aps.register(F32, val, self.cstp[:, i:i + 1])
        return val

    def use23(self, engine, name):
        if name in self.res23:
            for sem, val in self.res23[name]:
                if val is None:
                    self.p.wait_all(engine, sem)
                else:
                    self.p.wait(engine, sem, val)

    def free23(self, name, sem, val, sem2=None, val2=None):
        pairs = [(sem, val)]
        if sem2 is not None:
            pairs.append((sem2, val2))
        self.res23[name] = pairs

    def phase_sync(self, name):
        p = self.p
        for e in ENGINES:
            for s in (self.qld, self.qst, self.qgp):
                p.wait_all(e, s)
        p.barrier(name)

    def p0(self):
        p, nc = self.p, self.nc
        def memset_all(v):
            ins = v.memset(self.cstp[:, 15:16], 0.0)
            for val, i in self._cst.items():
                ins = v.memset(self.cstp[:, i:i + 1], val)
            return ins
        p.op("vector", memset_all, inc=self.dve)
        t_c = self.dve.n
        for e in ("scalar", "tensor", "gpsimd"):
            p.wait(e, self.dve, t_c)
        p.dma("sync", self.onescol_f[:], self.ones_c, self.qld)
        p.dma("sync", self.onesrow_f[:], self.ones_f[0:1, :], self.qld)
        p.dma("sync", self.identb[:], self.ident_b, self.qld)
        p.dma("sync", self.identf[:], self.ident_f, self.qld)
        p.dma("sync", self.wgv[:], self.wg, self.qld)
        p.dma("sync", self.bgv[:], self.bg, self.qld)
        p.dma("gpsimd", self.wq_sb.rearrange("p (ck q) -> p ck q", ck=CK),
              self.wq_p.rearrange("(ck p) q -> p ck q", p=128), self.qgp)
        p.dma("gpsimd", self.wk_sb.rearrange("p (ck q) -> p ck q", ck=CK),
              self.wk_p.rearrange("(ck p) q -> p ck q", p=128), self.qgp)
        p.dma("gpsimd", self.wv_sb.rearrange("p (ck q) -> p ck q", ck=CK),
              self.wv_p.rearrange("(ck p) q -> p ck q", p=128), self.qgp)
        t_ld = self.qld.n

        for j in range(KV):
            p.wait("sync", self.pe, self.pe.n)
            tok = p.dma("sync", self.wkT_sb, self.wkT_p[j * 128:(j + 1) * 128, :],
                        self.qld)
            p.wait("tensor", self.qld, tok)
            for ck in range(CK):
                tok_pe = p.op("tensor", lambda t, ck=ck: t.matmul(
                    self.ps[0][:, 0:1], self.wkT_sb[:, ck * 128:(ck + 1) * 128],
                    self.wgv[:], start=True, stop=True), inc=self.pe)
                p.wait("vector", self.pe, tok_pe)
                tok_dv = p.op("vector", lambda v, ck=ck, j=j: v.tensor_copy(
                    self.wg2[:, ck * 4 + j:ck * 4 + j + 1], self.ps[0][:, 0:1]),
                    inc=self.dve)
                p.wait("tensor", self.dve, tok_dv)
        tok_g2b = p.op("vector", lambda v: v.tensor_copy(self.wg2b[:], self.wg2[:]),
                       inc=self.dve)
        p.wait("tensor", self.dve, tok_g2b)

    def _sm_exp(self, bank, dst_raw, eqb):
        p = self.p
        tok_raw = p.op("scalar", lambda s: s.activation(
            self.ta[:], self.ps[bank][:, :], AF.Copy), inc=self.act)
        p.wait("scalar", self.act, tok_raw)
        p.dma("scalar", dst_raw, self.ta[:], self.qst)
        tok_eq = p.op("scalar", lambda s, eqb=eqb: s.activation(
            eqb, self.ps[bank][:, :], AF.Exp), inc=self.act)
        return tok_eq

    def _sm_ladder(self, tok_eq, eqb, dst_sm):
        p = self.p
        p.wait("tensor", self.act, tok_eq)
        tok_sum = p.op("tensor", lambda t, eqb=eqb: t.matmul(
            self.ps[2][0:1, :], self.onescol_f[:], eqb,
            start=True, stop=True), inc=self.pe)
        p.wait("vector", self.pe, tok_sum)
        tok_rc = p.op("vector", lambda v: v.reciprocal(self.rowa[:], self.ps[2][0:1, :]),
                      inc=self.dve)
        p.wait("tensor", self.dve, tok_rc)
        tok_bc = p.op("tensor", lambda t: t.matmul(
            self.ps[3][:, :], self.onesrow_f[0:1, 0:128], self.rowa[:],
            start=True, stop=True), inc=self.pe)
        p.wait("vector", self.pe, tok_bc)
        p.wait_all("vector", self.qst)
        tok_sm = p.op("vector", lambda v, eqb=eqb: v.tensor_tensor(
            self.tb[:], eqb, self.ps[3][:, :], op=ALU.mult), inc=self.dve)
        p.wait("scalar", self.dve, tok_sm)
        p.dma("scalar", dst_sm, self.tb[:], self.qst)
        return tok_sm

    def p1(self):
        p, nc = self.p, self.nc
        p.wait("tensor", self.qgp, self.qgp.n)
        p.wait("sync", self.pe, self.pe.n)
        x_r = self.x_fm.rearrange("(ck p) t -> p ck t", p=128)
        prev = {"xs_dve": 0, "xs_act": 0, "lns_pe": 0}
        for ts in range(NSL):
            cols = slice(ts * SL, (ts + 1) * SL)
            p.wait("sync", self.dve, prev["xs_dve"])
            p.wait("sync", self.act, prev["xs_act"])
            tok_x = p.dma("sync", self.xs[:].rearrange("p (ck t) -> p ck t", ck=CK),
                          x_r[:, :, cols], self.qld)
            p.wait("tensor", self.qld, tok_x)
            p.wait("scalar", self.qld, tok_x)
            for ck in range(CK):
                p.op("tensor", lambda t, ck=ck: t.matmul(
                    self.ps[2][0:1, :], self.onescol_f[:],
                    self.xs[:, ck * SL:(ck + 1) * SL],
                    start=(ck == 0), stop=(ck == CK - 1)),
                    inc=self.pe if ck == CK - 1 else None)
            sq_mm_tok = [0, 0]
            p.wait("scalar", self.dve, prev.get("sq_dve", 0))
            for ck in range(CK):
                sqb = self.sq[ck % 2]
                p.wait("scalar", self.pe, sq_mm_tok[ck % 2])
                tok_sq = p.op("scalar", lambda s, ck=ck, sqb=sqb: s.activation(
                    sqb[:], self.xs[:, ck * SL:(ck + 1) * SL], AF.Square),
                    inc=self.act)
                p.wait("tensor", self.act, tok_sq)
                tok_mm = p.op("tensor", lambda t, ck=ck, sqb=sqb: t.matmul(
                    self.ps[3][0:1, :], self.onescol_f[:], sqb[:],
                    start=(ck == 0), stop=(ck == CK - 1)), inc=self.pe)
                sq_mm_tok[ck % 2] = tok_mm
            t_stats = self.pe.n
            p.wait("vector", self.pe, t_stats)
            p.op("vector", lambda v: v.tensor_scalar(
                self.rowa[:], self.ps[2][0:1, :], 1.0 / C, None, op0=ALU.mult),
                inc=self.dve)
            p.op("vector", lambda v: v.tensor_scalar(
                self.rowb[:], self.ps[3][0:1, :], 1.0 / C, None, op0=ALU.mult),
                inc=self.dve)
            p.op("vector", lambda v: v.tensor_tensor(
                self.rowc[:], self.rowa[:], self.rowa[:], op=ALU.mult), inc=self.dve)
            tok_var = p.op("vector", lambda v: v.tensor_tensor(
                self.rowb[:], self.rowb[:], self.rowc[:], op=ALU.subtract), inc=self.dve)
            p.wait("scalar", self.dve, tok_var)
            p.op("scalar", lambda s: s.activation(self.rowb[:], self.rowb[:], AF.Ln,
                                                  bias=self.constf(1e-5)), inc=self.act)
            tok_rstd = p.op("scalar", lambda s: s.activation(
                self.rowb[:], self.rowb[:], AF.Exp, scale=self.constf(-0.5)), inc=self.act)
            p.wait("tensor", self.dve, tok_var)
            p.wait("tensor", self.act, tok_rstd)
            tok_mb = p.op("tensor", lambda t: t.matmul(
                self.ps[4][:, :], self.onesrow_f[0:1, 0:128], self.rowa[:],
                start=True, stop=True), inc=self.pe)
            tok_rb = p.op("tensor", lambda t: t.matmul(
                self.ps[5][:, :], self.onesrow_f[0:1, 0:128], self.rowb[:],
                start=True, stop=True), inc=self.pe)
            p.wait("vector", self.pe, tok_rb)
            p.wait("vector", self.pe, prev["lns_pe"])
            for ck in range(CK):
                sqb = self.sq[ck % 2]
                p.op("vector", lambda v, ck=ck, sqb=sqb: v.tensor_tensor(
                    sqb[:], self.xs[:, ck * SL:(ck + 1) * SL], self.ps[4][:, :],
                    op=ALU.subtract), inc=self.dve)
                p.op("vector", lambda v, ck=ck, sqb=sqb: v.tensor_tensor(
                    self.lns[:, ck * SL:(ck + 1) * SL], sqb[:], self.ps[5][:, :],
                    op=ALU.mult), inc=self.dve)
            t_ln = self.dve.n
            prev["xs_dve"] = t_ln
            p.wait("tensor", self.dve, t_ln)
            p.wait("scalar", self.dve, t_ln)

            eqs = [self.eq, self.sq[0]]
            pend = None
            lad_toks = [0, 0]
            for hq in range(HQ):
                bank = hq % 2
                eqb = eqs[hq % 2][:, 0:512]
                if hq >= 2:
                    p.wait("scalar", self.dve, lad_toks[hq % 2])
                for ck in range(CK):
                    tok_q = p.op("tensor", lambda t, ck=ck, hq=hq, bank=bank: t.matmul(
                        self.ps[bank][:, :],
                        self.wq_sb[:, ck * 1024 + hq * 128: ck * 1024 + (hq + 1) * 128],
                        self.lns[:, ck * SL:(ck + 1) * SL],
                        start=(ck == 0), stop=(ck == CK - 1)),
                        inc=self.pe if ck == CK - 1 else None)
                p.wait("scalar", self.pe, tok_q)
                p.wait_all("scalar", self.qst)
                tok_eq = self._sm_exp(bank, self.q_fm[hq * 128:(hq + 1) * 128, cols],
                                      eqb)
                if pend is not None:
                    lad_toks[(hq - 1) % 2] = self._sm_ladder(*pend)
                pend = (tok_eq, eqb, self.qh_fm[hq * 128:(hq + 1) * 128, cols])
                p.wait("tensor", self.act, tok_eq)
            lad_toks[(HQ - 1) % 2] = self._sm_ladder(*pend)

            pend = None
            lad_k = [0, 0]
            for j in range(KV + 1):
                if j < KV:
                    bank = j % 2
                    eqb = eqs[j % 2][:, 0:512]
                    p.wait("scalar", self.dve,
                           lad_k[j % 2] if j >= 2 else lad_toks[j % 2])
                    for ck in range(CK):
                        tok_k = p.op("tensor", lambda t, ck=ck, j=j, bank=bank: t.matmul(
                            self.ps[bank][:, :],
                            self.wk_sb[:, ck * 512 + j * 128: ck * 512 + (j + 1) * 128],
                            self.lns[:, ck * SL:(ck + 1) * SL],
                            start=(ck == 0), stop=(ck == CK - 1)),
                            inc=self.pe if ck == CK - 1 else None)
                    p.wait("scalar", self.pe, tok_k)
                    p.wait_all("scalar", self.qst)
                    tok_eq2 = self._sm_exp(bank, self.k_fm[j * 128:(j + 1) * 128, cols],
                                           eqb)
                    p.wait("tensor", self.act, tok_eq2)
                if pend is not None:
                    jp, peq, peqb = pend
                    lad_k[jp % 2] = self._sm_ladder(
                        peq, peqb, self.kh_fm[jp * 128:(jp + 1) * 128, cols])
                if j < KV:
                    pend = (j, tok_eq2, eqb)

            for su in range(4):
                bank = su % 2
                for ck in range(CK):
                    tok_v = p.op("tensor", lambda t, ck=ck, su=su, bank=bank: t.matmul(
                        self.ps[bank][:, :],
                        self.lns[:, ck * SL + su * 128: ck * SL + (su + 1) * 128],
                        self.wv_sb[:, ck * 512:(ck + 1) * 512],
                        start=(ck == 0), stop=(ck == CK - 1)),
                        inc=self.pe if ck == CK - 1 else None)
                p.wait("scalar", self.pe, tok_v)
                tok_ev = p.op("scalar", lambda s, bank=bank: s.activation(
                    self.ta[:], self.ps[bank][:, :], AF.Copy), inc=self.act)
                p.wait("scalar", self.act, tok_ev)
                st_v = p.dma("scalar",
                             self.v_tm[ts * SL + su * 128: ts * SL + (su + 1) * 128, :],
                             self.ta[:], self.qst)
                p.wait("scalar", self.qst, st_v)
                p.wait("tensor", self.act, tok_ev)

            for j in range(KV):
                for ck in range(CK):
                    tok_g = p.op("tensor", lambda t, ck=ck, j=j: t.matmul(
                        self.ps[2][0:1, :], self.wg2b[:, ck * 4 + j: ck * 4 + j + 1],
                        self.lns[:, ck * SL:(ck + 1) * SL],
                        start=(ck == 0), stop=(ck == CK - 1)),
                        inc=self.pe if ck == CK - 1 else None)
                p.wait("vector", self.pe, tok_g)
                p.op("vector", lambda v: v.tensor_scalar(
                    self.rowa[:], self.ps[2][0:1, :], self.bgv[0:1, 0:1], None,
                    op0=ALU.add), inc=self.dve)
                tok_z = self.dve.n
                p.wait("scalar", self.dve, tok_z)
                p.op("scalar", lambda s: s.activation(self.rowb[:], self.rowa[:],
                                                      AF.Abs), inc=self.act)
                p.op("scalar", lambda s: s.activation(self.rowb[:], self.rowb[:],
                                                      AF.Exp, scale=self.constf(-1.0)), inc=self.act)
                p.op("scalar", lambda s: s.activation(
                    self.rowb[:], self.rowb[:], AF.Ln, bias=1.0), inc=self.act)
                tok_rl = p.op("scalar", lambda s: s.activation(
                    self.rowc[:], self.rowa[:], AF.Relu), inc=self.act)
                p.wait("vector", self.act, tok_rl)
                p.op("vector", lambda v: v.tensor_tensor(
                    self.rowc[:], self.rowa[:], self.rowc[:], op=ALU.subtract),
                    inc=self.dve)
                tok_sub = p.op("vector", lambda v: v.tensor_tensor(
                    self.rowc[:], self.rowc[:], self.rowb[:], op=ALU.subtract),
                    inc=self.dve)
                p.wait("scalar", self.dve, tok_sub)
                p.wait_all("scalar", self.qst)
                tok_gs = p.op("scalar", lambda s: s.activation(
                    self.gsst[:], self.rowc[:], AF.Copy,
                    scale=self.constf(1.0 / 16.0)), inc=self.act)
                p.wait("scalar", self.act, tok_gs)
                p.dma("scalar", self.gs_d[j:j + 1, ts * SL:(ts + 1) * SL],
                      self.gsst[:], self.qst)
                p.wait("tensor", self.act, tok_gs)
                p.wait("vector", self.act, tok_gs)
            prev["lns_pe"] = self.pe.n
            prev["xs_act"] = self.act.n
            prev["sq_dve"] = self.dve.n



    def p2(self):
        p, nc, ctx = self.p, self.nc, self.ctx
        s_ = lambda n, sh, d: ctx.enter_context(nc.sbuf_tensor(n, sh, d))
        cur = [0]

        def carve(ncols):
            a = self.big1[:, cur[0]:cur[0] + ncols]
            cur[0] += ncols
            return a
        khf4 = [carve(512) for i in range(2)]
        khtm4 = [carve(512) for i in range(2)]
        vtm4 = [carve(512) for i in range(2)]
        qpf8 = [carve(1024) for i in range(2)]
        T128 = lambda n: [carve(128) for i in range(2)]
        Pb, mb = T128("p2Pb"), T128("p2mb")
        Kp, Km, Ktil = T128("p2Kp"), T128("p2Km"), T128("p2Ktil")
        Qp0, Qp1 = T128("p2Qp0"), T128("p2Qp1")
        Atl, Atu, MTt = T128("p2Atl"), T128("p2Atu"), T128("p2MTt")
        Xa, Xb = T128("p2Xa"), T128("p2Xb")
        XTa, XTb = T128("p2XTa"), T128("p2XTb")
        t1s = T128("p2t1s")
        Utm, Wtm, Mp = T128("p2Utm"), T128("p2Wtm"), T128("p2Mp")
        ftile, ftile2 = T128("p2ft"), T128("p2ft2")
        gcol = [s_(f"p2gcol{i}", [128, 1], F32) for i in range(2)]
        gsb = [[s_(f"p2gs{j}_{i}", [1, 128], F32) for i in range(2)] for j in range(KV)]
        crows = [s_(f"p2crows{i}", [1, 256], F32) for i in range(2)]
        trow = [s_(f"p2trow{i}", [1, 128], F32) for i in range(2)]
        dcol = [s_(f"p2dcol{i}", [128, 1], F32) for i in range(2)]
        gcc = [s_(f"p2gcc{i}", [128, 1], F32) for i in range(2)]
        Sf = [s_(f"p2Sf{j}", [128, 128], F32) for j in range(KV)]
        Sb = [s_(f"p2Sb{j}", [128, 128], BF16) for j in range(KV)]
        Stmp = [s_(f"p2Stmp{i}", [128, 128], F32) for i in range(2)]
        mls = self.sq[0][:, 0:128]
        mus = self.sq[0][:, 128:256]
        mui = self.sq[0][:, 256:384]

        qld, qst = self.qld, self.qst
        qs2 = [p.sem("qs2a"), p.sem("qs2b")]
        pe, dve, act = self.pe, self.dve, self.act
        kh_r = self.kh_fm.rearrange("(j p) t -> p j t", p=128)
        ktm_r = self.kh_tm.rearrange("j t d -> t j d")
        qh_r = self.qh_fm.rearrange("(h p) t -> p h t", p=128)
        FSC = float(SCALE * FLA_W)
        self.constf(FSC)
        self.constf(2.0)

        qmk = p.sem("qmk")
        p.dma("sync", mls[:], self.mask_ls, qmk)
        p.dma("sync", mus[:], self.mask_us, qmk)
        tok_m = p.dma("sync", mui[:], self.mask_ui, qmk)
        p.wait("vector", qmk, tok_m)
        p.wait("tensor", qmk, tok_m)

        for j in range(KV):
            p.op("vector", lambda v, j=j: v.memset(Sf[j][:], 0.0), inc=dve)
        t_s0 = dve.n
        p.wait("tensor", dve, t_s0)

        res = self.res23
        use, free = self.use23, self.free23

        def load_chunk(c):
            bp = c % 2
            use("sync", f"ldbuf{bp}")
            c0 = c * CH
            q_ = qs2[bp]
            p.dma("sync", khf4[bp].rearrange("p (j t) -> p j t", j=KV),
                  kh_r[:, :, c0:c0 + CH], q_)
            for j2 in range(KV):
                p.dma("sync", khtm4[bp][:, j2 * 128:(j2 + 1) * 128],
                      self.kh_fm[j2 * 128:(j2 + 1) * 128, c0:c0 + CH], q_,
                      transpose=True)
            p.dma("sync", vtm4[bp], self.v_tm[c0:c0 + CH, :], q_)
            p.dma("sync", qpf8[bp].rearrange("p (h t) -> p h t", h=HQ),
                  qh_r[:, :, c0:c0 + CH], q_)
            for j in range(KV):
                p.dma("sync", gsb[j][bp][:], self.gs_d[j:j + 1, c0:c0 + CH], q_)
            return q_.n

        def head_gen(c, j, t_ld):
            bp = c % 2
            pp = j % 2
            c0 = c * CH
            gb = 0
            psN = self.ps[1 + pp]
            psU = self.ps[3 + pp]
            psG = psN
            khf = khf4[bp][:, j * 128:(j + 1) * 128]
            khtm = khtm4[bp][:, j * 128:(j + 1) * 128]
            vtm = vtm4[bp][:, j * 128:(j + 1) * 128]
            qpf = [qpf8[bp][:, (2 * j + i) * 128:(2 * j + i + 1) * 128] for i in range(2)]

            use("tensor", f"psN{pp}")
            p.wait("tensor", qs2[bp], t_ld)
            tk = p.op("tensor", lambda t: t.matmul(
                psG[:, gb + 255:gb + 256], gsb[j][bp][:], self.onescol_f[0:1, 0:1],
                start=True, stop=True), inc=pe)
            p.wait("vector", pe, tk)
            use("vector", f"gcol{pp}")
            tk = p.op("vector", lambda v: v.tensor_copy(gcol[pp][:],
                                                        psG[:, gb + 255:gb + 256]), inc=dve)
            yield
            p.wait("tensor", dve, tk)
            p.op("tensor", lambda t: t.matmul(psG[0:1, gb:gb + 128], gcol[pp][:], mui[:],
                                              start=True, stop=True), inc=pe)
            tk = p.op("tensor", lambda t: t.matmul(psG[0:1, gb + 128:gb + 256], gcol[pp][:],
                                                   mls[:], start=True, stop=True), inc=pe)
            p.wait("vector", pe, tk)
            use("vector", f"crows{pp}")
            tk_cr = p.op("vector", lambda v: v.tensor_copy(crows[pp][:],
                                                           psG[0:1, gb:gb + 256]),
                         inc=dve)
            tk_tot = p.op("vector", lambda v: v.tensor_tensor(
                trow[pp][:], crows[pp][0:1, 0:128], crows[pp][0:1, 128:256], op=ALU.add),
                inc=dve)
            yield
            p.wait("tensor", dve, tk_tot)
            p.op("tensor", lambda t: t.matmul(psG[:, gb:gb + 1], crows[pp][0:1, 128:256],
                                              self.onescol_f[0:1, 0:1], start=True, stop=True),
                 inc=pe)
            p.op("tensor", lambda t: t.matmul(psG[:, gb + 1:gb + 2], trow[pp][:],
                                              self.onescol_f[0:1, 0:1], start=True, stop=True),
                 inc=pe)
            tk = p.op("tensor", lambda t: t.matmul(psG[:, gb + 128:gb + 256],
                                                   self.onesrow_f[0:1, 0:128],
                                                   crows[pp][0:1, 0:128], start=True, stop=True),
                      inc=pe)
            p.wait("scalar", pe, tk)
            use("scalar", f"dcol{pp}")
            p.op("scalar", lambda s: s.activation(dcol[pp][:], psG[:, gb:gb + 1], AF.Exp),
                 inc=act)
            p.op("scalar", lambda s: s.activation(gcc[pp][:], psG[:, gb + 1:gb + 2], AF.Exp),
                 inc=act)
            use("scalar", f"Pb{pp}")
            p.op("scalar", lambda s: s.activation(Pb[pp], psG[:, gb + 128:gb + 256], AF.Exp),
                 inc=act)
            tk_g = p.op("scalar", lambda s: s.activation(
                mb[pp], psG[:, gb + 128:gb + 256], AF.Exp, scale=self.constf(-1.0)), inc=act)
            free(f"gcol{pp}", pe, pe.n)
            free(f"crows{pp}", pe, pe.n)
            yield
            p.wait("vector", act, tk_g)
            p.wait("vector", qs2[bp], t_ld)
            use("vector", f"Kp{pp}")
            p.op("vector", lambda v: v.tensor_tensor(Kp[pp], khf, Pb[pp], op=ALU.mult),
                 inc=dve)
            p.op("vector", lambda v: v.tensor_tensor(Km[pp], khf, mb[pp], op=ALU.mult),
                 inc=dve)
            p.op("vector", lambda v: v.tensor_scalar(Ktil[pp], khtm, dcol[pp][:], None,
                                                     op0=ALU.mult), inc=dve)
            p.op("vector", lambda v: v.tensor_tensor(Qp0[pp], qpf[0], Pb[pp], op=ALU.mult),
                 inc=dve)
            tk_kp = p.op("vector", lambda v: v.tensor_tensor(Qp1[pp], qpf[1], Pb[pp],
                                                             op=ALU.mult), inc=dve)
            free(f"Pb{pp}", dve, tk_kp)
            free(f"dcol{pp}", dve, tk_kp)
            yield
            p.wait("tensor", dve, tk_kp)
            use("tensor", f"psN{pp}")
            tk = p.op("tensor", lambda t: t.matmul(psN[:, 0:128], Kp[pp], Km[pp],
                                                   start=True, stop=True), inc=pe)
            p.wait("vector", pe, tk)
            use("vector", f"Atl{pp}")
            tk_atl = p.op("vector", lambda v: v.tensor_tensor(Atl[pp], psN[:, 0:128], mls[:],
                                                              op=ALU.mult), inc=dve)
            p.wait("tensor", dve, tk_atl)
            tk = p.op("tensor", lambda t: t.matmul(psN[:, 0:128], Km[pp], Kp[pp],
                                                   start=True, stop=True), inc=pe)
            p.wait("vector", pe, tk)
            tk_atu = p.op("vector", lambda v: v.tensor_tensor(Atu[pp], psN[:, 0:128], mus[:],
                                                              op=ALU.mult), inc=dve)
            p.op("vector", lambda v: v.tensor_tensor(Xa[pp], self.identb[:], Atl[pp],
                                                     op=ALU.subtract), inc=dve)
            p.op("vector", lambda v: v.tensor_tensor(XTa[pp], self.identb[:], Atu[pp],
                                                     op=ALU.subtract), inc=dve)
            tk_mt = p.op("vector", lambda v: v.tensor_tensor(MTt[pp], self.identb[:],
                                                             Atu[pp], op=ALU.add), inc=dve)
            free(f"Atl{pp}", dve, tk_mt)
            yield
            X, XT = Xa[pp], XTa[pp]
            Xn, XTn = Xb[pp], XTb[pp]
            p.wait("tensor", dve, tk_mt)
            for it in range(6):
                tk = p.op("tensor", lambda t, X=X: t.matmul(psN[:, 0:128], MTt[pp], X,
                                                            start=True, stop=True), inc=pe)
                p.wait("scalar", pe, tk)
                tk_t1 = p.op("scalar", lambda s: s.activation(t1s[pp], psN[:, 0:128],
                                                              AF.Copy), inc=act)
                p.wait("tensor", act, tk_t1)
                tk = p.op("tensor", lambda t, XT=XT: t.matmul(psN[:, 128:256], XT, t1s[pp],
                                                              start=True, stop=True), inc=pe)
                p.wait("vector", pe, tk)
                tk_x = p.op("vector", lambda v, X=X, Xn=Xn: v.scalar_tensor_tensor(
                    Xn, X, 2.0, psN[:, 128:256], op0=ALU.mult, op1=ALU.subtract), inc=dve)
                p.wait("tensor", dve, tk_x)
                tk = p.op("tensor", lambda t, XT=XT: t.matmul(psN[:, 256:384], t1s[pp], XT,
                                                              start=True, stop=True), inc=pe)
                p.wait("vector", pe, tk)
                tk_xt = p.op("vector", lambda v, XT=XT, XTn=XTn: v.scalar_tensor_tensor(
                    XTn, XT, 2.0, psN[:, 256:384], op0=ALU.mult, op1=ALU.subtract), inc=dve)
                p.wait("tensor", dve, tk_xt)
                X, Xn = Xn, X
                XT, XTn = XTn, XT
                yield
            free(f"psN{pp}", dve, dve.n)
            use("tensor", f"psU{pp}")
            if c == 0:
                Uop = vtm
            else:
                use("tensor", f"Sb{j}")
                tk = p.op("tensor", lambda t: t.matmul(psU[:, 0:128], Kp[pp], Sb[j][:],
                                                       start=True, stop=True), inc=pe)
                p.wait("vector", pe, tk)
                use("vector", f"Utm{pp}")
                tk_u = p.op("vector", lambda v: v.tensor_tensor(Utm[pp], vtm, psU[:, 0:128],
                                                                op=ALU.subtract), inc=dve)
                p.wait("tensor", dve, tk_u)
                Uop = Utm[pp]
                yield
            tk = p.op("tensor", lambda t, XT=XT, Uop=Uop: t.matmul(
                psU[:, 0:128], XT, Uop, start=True, stop=True), inc=pe)
            p.wait("scalar", pe, tk)
            use("scalar", f"Wtm{pp}")
            tk_w = p.op("scalar", lambda s: s.activation(Wtm[pp], psU[:, 0:128], AF.Copy),
                        inc=act)
            p.wait("tensor", act, tk_w)
            yield
            tk_f = None
            for i in range(2):
                Qpi = (Qp0, Qp1)[i][pp]
                use("tensor", f"psN{pp}")
                tk = p.op("tensor", lambda t, Qpi=Qpi: t.matmul(psN[:, 0:128], Km[pp], Qpi,
                                                                start=True, stop=True), inc=pe)
                p.wait("vector", pe, tk)
                use("vector", f"Mp{pp}")
                tk_mp = p.op("vector", lambda v: v.tensor_tensor(Mp[pp], psN[:, 0:128],
                                                                 mui[:], op=ALU.mult), inc=dve)
                free(f"psN{pp}", dve, tk_mp)
                p.wait("tensor", dve, tk_mp)
                if tk_f is not None:
                    p.wait("tensor", act, tk_f)
                if c == 0:
                    tk = p.op("tensor", lambda t: t.matmul(psU[:, 128:256], Mp[pp],
                                                           Wtm[pp], start=True, stop=True),
                              inc=pe)
                else:
                    p.op("tensor", lambda t, Qpi=Qpi: t.matmul(psU[:, 128:256], Qpi, Sb[j][:],
                                                               start=True, stop=False), inc=pe)
                    tk = p.op("tensor", lambda t: t.matmul(psU[:, 128:256], Mp[pp],
                                                           Wtm[pp], start=False, stop=True),
                              inc=pe)
                p.wait("scalar", pe, tk)
                use("scalar", f"ftile{pp}")
                tk_f = p.op("scalar", lambda s: s.activation(ftile[pp], psU[:, 128:256],
                                                             AF.Copy, scale=FSC), inc=act)
                p.wait("scalar", act, tk_f)
                shard = c0 // TC
                cc = c0 % TC
                hq = 2 * j + i
                p.dma("scalar",
                      self.contrib[shard * TC + cc: shard * TC + cc + CH,
                                   hq * 64:(hq + 1) * 64],
                      ftile[pp].bitcast(F32), qst)
                free(f"ftile{pp}", qst, None)
                yield
            tk = p.op("tensor", lambda t: t.matmul(psU[:, 0:128], Ktil[pp], Wtm[pp],
                                                   start=True, stop=True), inc=pe)
            free(f"Wtm{pp}", pe, pe.n)
            free(f"Kp{pp}", pe, pe.n)
            free(f"Mp{pp}", pe, pe.n)
            free(f"Utm{pp}", pe, pe.n)
            p.wait("vector", pe, tk)
            if c == 0:
                tk_s = p.op("vector", lambda v: v.tensor_copy(Sf[j][:], psU[:, 0:128]), inc=dve)
            else:
                p.op("vector", lambda v: v.tensor_scalar(Stmp[pp][:], Sf[j][:], gcc[pp][:],
                                                         None, op0=ALU.mult), inc=dve)
                tk_s = p.op("vector", lambda v: v.tensor_tensor(Sf[j][:], Stmp[pp][:],
                                                                psU[:, 0:128], op=ALU.add),
                            inc=dve)
            free(f"psU{pp}", dve, tk_s, act, tk_f)
            p.wait("scalar", dve, tk_s)
            tk_sb = p.op("scalar", lambda s: s.activation(Sb[j][:], Sf[j][:], AF.Copy), inc=act)
            res[f"Sb{j}"] = [(act, tk_sb)]
            yield

        p.wait("sync", qst, qst.n)
        t_ld = load_chunk(0)
        for c in range(NCH):
            if c + 1 < NCH:
                t_ld_next = load_chunk(c + 1)
            for jj in range(0, KV, 2):
                gens = [head_gen(c, jj, t_ld), head_gen(c, jj + 1, t_ld)]
                done = [False, False]
                for _ in range(3):
                    try:
                        next(gens[0])
                    except StopIteration:
                        done[0] = True
                        break
                    else:
                        yield
                while not all(done):
                    for gi in range(2):
                        if not done[gi]:
                            try:
                                next(gens[gi])
                            except StopIteration:
                                done[gi] = True
                            else:
                                yield
            free(f"ldbuf{c % 2}", self.pe, self.pe.n)
            if c + 1 < NCH:
                t_ld = t_ld_next


    def p3(self):
        p, nc, ctx = self.p, self.nc, self.ctx
        qst = self.qst
        pe, dve, act = self.pe, self.dve, self.act
        res = self.res23
        use, free = self.use23, self.free23
        q3 = [p.sem("q3a"), p.sem("q3b")]
        qq3 = [p.sem("qq3a"), p.sem("qq3b")]
        B3 = 16384
        Kres = self.big1[:, B3:B3 + 8192][:, 0:4096]
        Kres = self.big1[:, B3:B3 + 4096]
        vbuf = self.big1[:, B3 + 4096:B3 + 8224]
        vb3 = vbuf.rearrange("p (n e) -> p n e", e=129)
        Qt = [self.big1[:, B3 + 8224 + i * 512: B3 + 8224 + (i + 1) * 512]
              for i in range(2)]
        Pt = [self.big1[:, B3 + 9248 + i * 512: B3 + 9248 + (i + 1) * 512]
              for i in range(2)]
        ot = self.big1[:, B3 + 10272:B3 + 10400]
        ot2 = self.big1[:, B3 + 10400:B3 + 10528]
        cm_sb = self.xs[:, 2048:4096]
        rcol = self.ctx.enter_context(nc.sbuf_tensor("p3rcol", [128, 1], F32))

        qcm = p.sem("qcm")
        tok_cm = p.dma("sync", cm_sb.rearrange("p (s q) -> p s q", s=4),
                       self.cmask.rearrange("s p q -> p s q"), qcm)
        p.wait("vector", qcm, tok_cm)
        tok_on = p.op("vector", lambda v: v.memset(vb3[:, :, 128:129], 1.0))
        p.wait("sync", dve, tok_on)

        for j in range(KV):
            jq = q3[j % 2]
            use("sync", "kv")
            p.dma("sync", Kres, self.k_fm[j * 128:(j + 1) * 128, :], jq)
            p.dma("sync", vb3[:, :, 0:128],
                  self.v_tm[:, j * 128:(j + 1) * 128].rearrange("(n p) d -> p n d", p=128),
                  jq)
            t_kv = jq.n
            p.wait("tensor", jq, t_kv)
            yield
            for i in range(2):
                hq = 2 * j + i
                for qg in range(8):
                    qp = (hq * 8 + qg) % 2
                    use("sync", f"qt{qp}")
                    qqs = qq3[qp]
                    p.dma("sync", Qt[qp], self.q_fm[hq * 128:(hq + 1) * 128,
                                                    qg * 512:(qg + 1) * 512], qqs)
                    t_q = qqs.n
                    p.wait("tensor", qqs, t_q)
                    nkt = 4 * qg + 4

                    pscs = [self.ps[5], self.ps[0]]

                    def score_mm(kt):
                        psc = pscs[kt % 2]
                        use("tensor", f"psc{kt % 2}")
                        return p.op("tensor", lambda t, kt=kt, qp=qp, psc=psc: t.matmul(
                            psc[:, :], Kres[:, kt * 128:(kt + 1) * 128], Qt[qp],
                            start=True, stop=True), inc=pe)
                    tok_sc = score_mm(0)
                    for kt in range(nkt):
                        sp = kt % 2
                        psc = pscs[kt % 2]
                        p.wait("scalar", pe, tok_sc)
                        s_ = kt - 4 * qg
                        use("scalar", f"Pt{sp}")
                        tk_e = p.op("scalar", lambda s, sp=sp, psc=psc: s.activation(
                            Pt[sp], psc[:, :], AF.Exp, scale=self.constf(SCALE)), inc=act)
                        free(f"psc{kt % 2}", act, tk_e)
                        if s_ < 0:
                            p.wait("tensor", act, tk_e)
                            t_pt = tk_e
                        else:
                            p.wait("vector", act, tk_e)
                            tk_m = p.op("vector", lambda v, sp=sp, s_=s_: v.tensor_tensor(
                                Pt[sp], Pt[sp], cm_sb[:, s_ * 512:(s_ + 1) * 512],
                                op=ALU.mult), inc=dve)
                            p.wait("tensor", dve, tk_m)
                            t_pt = tk_m
                        if kt + 1 < nkt:
                            tok_sc = score_mm(kt + 1)
                        for su in range(4):
                            if kt > 4 * qg + su:
                                continue
                            po = self.ps[6 + su // 2][:, 129 * (su % 2):129 * (su % 2) + 129]
                            if kt == 0 and su % 2 == 0:
                                use("tensor", f"po{su // 2}")
                            tk = p.op("tensor", lambda t, su=su, sp=sp, po=po, kt=kt,
                                      qg=qg: t.matmul(
                                po[:, 0:129], Pt[sp][:, su * 128:(su + 1) * 128],
                                vb3[:, kt, :], start=(kt == 0 and su % 2 == 0),
                                stop=(su % 2 == 1 and kt == 4 * qg + su)), inc=pe)
                        free(f"Pt{sp}", pe, pe.n)
                    free(f"qt{qp}", pe, pe.n)
                    for su in range(4):
                        po = self.ps[6 + su // 2][:, 129 * (su % 2):129 * (su % 2) + 129]
                        p.wait("vector", pe, pe.n)
                        tk_r = p.op("vector", lambda v, po=po: v.reciprocal(
                            rcol[:], po[:, 128:129]), inc=dve)
                        use("vector", "ot")
                        tk_o = p.op("vector", lambda v, po=po: v.tensor_scalar(
                            ot, po[:, 0:128], rcol[:], None, op0=ALU.mult),
                            inc=dve)
                        if su % 2 == 1:
                            free(f"po{su // 2}", dve, tk_o)
                        p.wait("scalar", dve, tk_o)
                        qcol = qg * 512 + su * 128
                        shard = qcol // TC
                        cc = qcol % TC
                        p.dma("scalar",
                              self.contrib[shard * TC + cc: shard * TC + cc + 128,
                                           512 + hq * 64: 512 + (hq + 1) * 64],
                              ot.bitcast(F32), qst)
                        free("ot", qst, None)
                    yield
            free("kv", self.pe, self.pe.n)


    def p23(self):
        self.res23 = {}
        g2 = self.p2()
        g3 = self.p3()
        d2 = d3 = False
        while not (d2 and d3):
            for _ in range(12):
                if not d2:
                    try:
                        next(g2)
                    except StopIteration:
                        d2 = True
            if not d3:
                try:
                    next(g3)
                except StopIteration:
                    d3 = True

    def p4(self):
        p = self.p
        cc = p.sem("ccol")
        p.wait_all("gpsimd", self.qst)
        tok = p.op("gpsimd", lambda g: g.collective_compute(
            "AllGather", ALU.bypass,
            replica_groups=[[0, 1], [2, 3], [4, 5], [6, 7]],
            ins=[self.contrib], outs=[self.gath]), inc=cc)
        for e in ("sync", "tensor", "gpsimd"):
            p.wait(e, cc, tok)

    def ln_stream(self, src, dst_sbuf, dbg_out=None):
        p, nc = self.p, self.nc
        pe, dve, act = self.pe, self.dve, self.act
        qln = [p.sem("qln0"), p.sem("qln1")]
        src_r = src.rearrange("(ck p) t -> p ck t", p=128)
        xld = [self.xs[:, i * 512:(i + 1) * 512] for i in range(2)]
        ld_free = [0, 0]
        sq_free = [0]
        for tt in range(4):
            cols = slice(tt * 512, (tt + 1) * 512)
            for pas in ("stats", "norm"):
                for ck in range(CK):
                    bp = ck % 2
                    p.wait("sync", pe, ld_free[bp])
                    tok_x = p.dma("sync", xld[bp], src_r[:, ck, cols], qln[bp])
                    p.wait("tensor", qln[bp], tok_x)
                    if pas == "stats":
                        p.op("tensor", lambda t, bp=bp, ck=ck: t.matmul(
                            self.ps[2][0:1, :], self.onescol_f[:], xld[bp],
                            start=(ck == 0), stop=(ck == CK - 1)),
                            inc=pe if ck == CK - 1 else None)
                        p.wait("scalar", qln[bp], tok_x)
                        p.wait("scalar", pe, sq_free[0])
                        tok_sq = p.op("scalar", lambda s, bp=bp: s.activation(
                            self.sq[1][:], xld[bp], AF.Square), inc=act)
                        p.wait("tensor", act, tok_sq)
                        tok_mm = p.op("tensor", lambda t, ck=ck: t.matmul(
                            self.ps[3][0:1, :], self.onescol_f[:], self.sq[1][:],
                            start=(ck == 0), stop=(ck == CK - 1)), inc=pe)
                        ld_free[bp] = tok_mm
                        sq_free[0] = tok_mm
                    else:
                        p.wait("vector", qln[bp], tok_x)
                        p.op("vector", lambda v, bp=bp: v.tensor_tensor(
                            self.eq[:], xld[bp], self.ps[4][:, :], op=ALU.subtract),
                            inc=dve)
                        tok_n = p.op("vector", lambda v, ck=ck, tt=tt: v.tensor_tensor(
                            dst_sbuf[:, ck * TC + tt * 512: ck * TC + (tt + 1) * 512],
                            self.eq[:], self.ps[5][:, :], op=ALU.mult), inc=dve)
                        self.streams_pl = None
                        ld_free[bp] = 0
                        p.wait("sync", dve, tok_n)
                        if dbg_out is not None:
                            p.wait("scalar", dve, tok_n)
                            p.dma("scalar",
                                  dbg_out.rearrange("(c2 p) t -> p c2 t", p=128)[:, ck, cols],
                                  dst_sbuf[:, ck * TC + tt * 512: ck * TC + (tt + 1) * 512],
                                  self.qst)
                if pas == "stats":
                    t_stats = pe.n
                    p.wait("vector", pe, t_stats)
                    p.op("vector", lambda v: v.tensor_scalar(
                        self.rowa[:], self.ps[2][0:1, :], 1.0 / C, None, op0=ALU.mult),
                        inc=dve)
                    p.op("vector", lambda v: v.tensor_scalar(
                        self.rowb[:], self.ps[3][0:1, :], 1.0 / C, None, op0=ALU.mult),
                        inc=dve)
                    p.op("vector", lambda v: v.tensor_tensor(
                        self.rowc[:], self.rowa[:], self.rowa[:], op=ALU.mult), inc=dve)
                    tok_var = p.op("vector", lambda v: v.tensor_tensor(
                        self.rowb[:], self.rowb[:], self.rowc[:], op=ALU.subtract),
                        inc=dve)
                    p.wait("scalar", dve, tok_var)
                    p.op("scalar", lambda s: s.activation(
                        self.rowb[:], self.rowb[:], AF.Ln, bias=self.constf(1e-5)),
                        inc=act)
                    tok_rstd = p.op("scalar", lambda s: s.activation(
                        self.rowb[:], self.rowb[:], AF.Exp, scale=self.constf(-0.5)),
                        inc=act)
                    p.wait("tensor", dve, tok_var)
                    p.wait("tensor", act, tok_rstd)
                    p.op("tensor", lambda t: t.matmul(
                        self.ps[4][:, :], self.onesrow_f[0:1, 0:128], self.rowa[:],
                        start=True, stop=True), inc=pe)
                    tok_rb = p.op("tensor", lambda t: t.matmul(
                        self.ps[5][:, :], self.onesrow_f[0:1, 0:128], self.rowb[:],
                        start=True, stop=True), inc=pe)
                    p.wait("vector", pe, tok_rb)
            p.wait("tensor", self.dve, self.dve.n)

    def p5(self):
        p, nc = self.p, self.nc
        pe, dve, act, qld, qst, qgp = (self.pe, self.dve, self.act, self.qld,
                                       self.qst, self.qgp)
        attn_res = self.big1[:, 0:32768]
        fla_t = [self.ta[:], self.tb[:]]
        blA = self.lns[:, 4096:6144]
        blB = self.lns[:, 6144:8192]
        blB2 = [self.lns[:, 4096 + i * 512:4096 + (i + 1) * 512] for i in range(2)]
        wo_sl = [self.lns[:, i * 2048:(i + 1) * 2048] for i in range(2)]
        x1t = self.xs[:, 1024:1536]
        xld = [self.xs[:, i * 512:(i + 1) * 512] for i in range(2)]
        x_r = self.x_own.rearrange("(ck p) t -> p ck t", p=128)
        wo_r = self.wo.rearrange("(ck p) d -> p ck d", p=128)
        psl = self.ctx.enter_context(self.nc.sbuf_tensor("psl", [128, 2], F32))
        q5 = [p.sem("q5a"), p.sem("q5b")]
        qw5 = [p.sem("qw5a"), p.sem("qw5b")]
        qpsl = p.sem("qpsl")
        tok_ps = p.dma("sync", psl[:], self.psel, qpsl)
        p.wait("vector", qpsl, tok_ps)

        gb_ = self.gath.bitcast(BF16)
        ar3 = attn_res.rearrange("p (hd t) -> p hd t", hd=CK)
        t_bl = 0
        bl_tok = [0] * CK
        for hd in range(CK):
            r_, hq = hd // 8, hd % 8
            p.wait("sync", self.dve, t_bl)
            tA = p.dma("sync", blA,
                       gb_[r_ * 2 * C: r_ * 2 * C + TC,
                           1024 + hq * 128: 1024 + (hq + 1) * 128], qld,
                       transpose=True)
            tB = p.dma("sync", blB,
                       gb_[r_ * 2 * C + TC: r_ * 2 * C + 2 * TC,
                           1024 + hq * 128: 1024 + (hq + 1) * 128], qld,
                       transpose=True)
            p.wait("vector", qld, tB)
            p.op("vector", lambda v: v.tensor_scalar(
                blA, blA, psl[:, 0:1], None, op0=ALU.mult), inc=dve)
            t_bl = p.op("vector", lambda v, hd=hd: v.scalar_tensor_tensor(
                ar3[:, hd, :], blB, psl[:, 1:2], blA, op0=ALU.mult, op1=ALU.add),
                inc=dve)
            bl_tok[hd] = t_bl
        p.wait("sync", self.dve, t_bl)
        wo_free = [0, 0]
        xld_free = [0, 0]
        bank_free = [0, 0]
        for dc in range(CK):
            wp = dc % 2
            p.wait("gpsimd", pe, wo_free[wp])
            tok_w = p.dma("gpsimd", wo_sl[wp].rearrange("p (ck d) -> p ck d", ck=CK),
                          wo_r[:, :, dc * 128:(dc + 1) * 128], qw5[wp])
            p.wait("tensor", qw5[wp], tok_w)
            for tt in range(4):
                bank = tt % 2
                p.wait("tensor", dve, bank_free[bank])
                for hd in range(CK):
                    if dc == 0 and tt == 0:
                        p.wait("tensor", dve, bl_tok[hd])
                    tok_mm = p.op("tensor", lambda t, hd=hd, tt=tt, bank=bank, wp=wp: t.matmul(
                        self.ps[bank][:, :], wo_sl[wp][:, hd * 128:(hd + 1) * 128],
                        attn_res[:, hd * TC + tt * 512: hd * TC + (tt + 1) * 512],
                        start=(hd == 0), stop=(hd == CK - 1)),
                        inc=pe if hd == CK - 1 else None)
                xp = tt % 2
                p.wait("sync", dve, xld_free[xp])
                tok_x = p.dma("sync", xld[xp], x_r[:, dc, tt * 512:(tt + 1) * 512], q5[xp])
                p.dma("sync", fla_t[xp][:, 0:512],
                      gb_[(dc // 8) * 2 * C + tt * 512:
                          (dc // 8) * 2 * C + (tt + 1) * 512,
                          (dc % 8) * 128:(dc % 8 + 1) * 128], q5[xp],
                      transpose=True)
                p.dma("sync", blB2[xp],
                      gb_[(dc // 8) * 2 * C + TC + tt * 512:
                          (dc // 8) * 2 * C + TC + (tt + 1) * 512,
                          (dc % 8) * 128:(dc % 8 + 1) * 128], q5[xp],
                      transpose=True)
                tok_x = q5[xp].n
                p.wait("vector", pe, tok_mm)
                p.wait("vector", q5[xp], tok_x)
                p.op("vector", lambda v, xp=xp: v.tensor_scalar(
                    fla_t[xp], fla_t[xp], psl[:, 0:1], None, op0=ALU.mult), inc=dve)
                p.op("vector", lambda v, xp=xp: v.scalar_tensor_tensor(
                    fla_t[xp], blB2[xp], psl[:, 1:2], fla_t[xp],
                    op0=ALU.mult, op1=ALU.add), inc=dve)
                tok_c = p.op("vector", lambda v, bank=bank, xp=xp: v.scalar_tensor_tensor(
                    x1t[:], self.ps[bank][:, :], 0.99, fla_t[xp],
                    op0=ALU.mult, op1=ALU.add), inc=dve)
                bank_free[bank] = tok_c
                tok_x1 = p.op("vector", lambda v, xp=xp: v.tensor_tensor(
                    x1t[:], x1t[:], xld[xp], op=ALU.add), inc=dve)
                xld_free[xp] = tok_x1
                p.wait("scalar", dve, tok_x1)
                p.dma("scalar", self.x1r.rearrange("(ck p) t -> p ck t", p=128)
                      [:, dc, tt * 512:(tt + 1) * 512], x1t[:], qst)
                p.wait_all("vector", qst)
            wo_free[wp] = pe.n
        p.wait_all("sync", qst)
        self.ln_stream(self.x1r, self.big0,
                       dbg_out=self.ln_d if "ln_d" in self.dbg else None)


    def p0b(self):
        p, nc = self.p, self.nc
        qc = [p.sem("qc0"), p.sem("qc1")]
        bc = [self.big1[:, 0:16384], self.big1[:, 16384:32768]]
        jobs = []
        for e in range(8):
            s_ = self.wguI[e].rearrange("(ck p) d -> p ck d", p=128)
            d_ = self.wguC[e].rearrange("(ck p) d -> p ck d", p=128)
            jobs.append((s_[:, :, 0:1024], d_[:, :, 0:1024], (CK, 1024)))
            jobs.append((s_[:, :, 1024:2048], d_[:, :, 1024:2048], (CK, 1024)))
        for e in range(8):
            s_ = self.wdT[e].rearrange("(hk p) d -> p hk d", p=128)
            d_ = self.wdC[e].rearrange("(hk p) d -> p hk d", p=128)
            jobs.append((s_, d_, (8, 2048)))
        sg_ = self.shguI.rearrange("(ck p) d -> p ck d", p=128)
        dg_ = self.shguC.rearrange("(ck p) d -> p ck d", p=128)
        for q in range(4):
            jobs.append((sg_[:, :, q * 1024:(q + 1) * 1024],
                         dg_[:, :, q * 1024:(q + 1) * 1024], (CK, 1024)))
        sd_ = self.shdT.rearrange("(hk p) d -> p hk d", p=128)
        dd_ = self.shdC.rearrange("(hk p) d -> p hk d", p=128)
        for q in range(2):
            jobs.append((sd_[:, :, q * 1024:(q + 1) * 1024],
                         dd_[:, :, q * 1024:(q + 1) * 1024], (CK, 1024)))
        tok = [0, 0]
        for i, (s_, d_, (n1, n2)) in enumerate(jobs):
            bp = i % 2
            buf = bc[bp][:, 0:n1 * n2].rearrange("p (a b) -> p a b", a=n1)
            p.wait("gpsimd", qc[bp], tok[bp])
            tl = p.dma("gpsimd", buf, s_, qc[bp])
            p.wait("gpsimd", qc[bp], tl)
            tok[bp] = p.dma("gpsimd", buf if False else d_, buf, qc[bp]) or qc[bp].n
        self.tok_cast_done = [qc[0].n, qc[1].n]
        self.qc = qc

    def cast_wait(self):
        for e in ENGINES:
            self.p.wait(e, self.qc[0], self.tok_cast_done[0])
            self.p.wait(e, self.qc[1], self.tok_cast_done[1])

    def ffn(self, src_res, dst_res, tag):
        p, nc = self.p, self.nc
        pe, dve, act, qld, qst, qgp = (self.pe, self.dve, self.act, self.qld,
                                       self.qst, self.qgp)
        lnb = self.big0
        cbc = [self.lns[:, e * 512:(e + 1) * 512] for e in range(8)]
        wd_sl = [self.lns[:, 4096 + i * 1024: 4096 + (i + 1) * 1024] for i in range(2)]
        sgb = self.lns[:, 6144:6656]
        gta = [self.lns[:, 6656 + i * 512: 6656 + (i + 1) * 512] for i in range(2)]
        ish_sb = self.xs[:, 1536:2560]
        xstr = [self.xs[:, 2560 + i * 512:2560 + (i + 1) * 512] for i in range(2)]
        xtmp = self.xs[:, 3584:4096]
        wgu = self.xs[:, 4096:8192].bitcast(BF16)
        gu_sl = [wgu[:, i * 4096:(i + 1) * 4096] for i in range(2)]
        gw_sb = self.sq[0][:, 384:448].bitcast(BF16)
        sc_em = self.eq[0:8, :]
        sc_tm = self.sq[0][:, 0:32]
        cb_tm = self.sq[0][:, 32:64]
        m1c = self.sq[0][:, 64:65]
        m2c = self.sq[0][:, 65:66]
        w1c = self.sq[0][:, 66:67]
        w2c = self.sq[0][:, 67:68]
        msk = self.sq[0][:, 68:100]
        qg6 = [p.sem(tag + "qg0"), p.sem(tag + "qg1")]
        qd6 = [p.sem(tag + "qd0"), p.sem(tag + "qd1")]
        q6x = [p.sem(tag + "qx0"), p.sem(tag + "qx1")]
        qmisc = p.sem(tag + "qm")

        tok_i = p.dma("sync", ish_sb, self.ishift, qmisc)
        tok_g = p.dma("gpsimd", gw_sb.rearrange("p (ck e) -> p ck e", ck=CK),
                      self.gwT.rearrange("(ck p) e -> p ck e", p=128), qgp)
        seg_sb = self.sq[0][:, 100:108].bitcast(BF16)
        tok_s = p.dma("gpsimd", seg_sb, self.segR, qgp)
        p.wait("tensor", qgp, tok_s)
        p.wait("tensor", qmisc, tok_i)
        src_r = src_res.rearrange("(ck p) t -> p ck t", p=128)
        dst_r = dst_res.rearrange("(ck p) t -> p ck t", p=128)
        wguC_r = self.wguC.rearrange("e (ck p) d -> e p ck d", p=128)
        wdC_r = self.wdC.rearrange("e (hk p) d -> e p hk d", p=128)
        shgu_r = self.shguC.rearrange("(ck p) d -> p ck d", p=128)
        shd_r = self.shdC.rearrange("(hk p) d -> p hk d", p=128)

        gu_free = [0, 0]
        gubA = [0, 0, 0, 0]
        gubD = [0, 0, 0, 0]
        gta_free = [0, 0]
        wd_free = [0, 0]
        x_free = [0, 0]
        dbank_free = [0, 0]

        def gate_up(dst_slot, wsrc, gi, cbe, lnt):
            wp = gi % 2
            p.wait("sync", pe, gu_free[wp])
            tok_w = p.dma("sync", gu_sl[wp].rearrange("p (ck d) -> p ck d", ck=CK),
                          wsrc, qg6[wp])
            p.wait("tensor", qg6[wp], tok_w)
            b0 = 2 * (gi % 2)
            b1 = b0 + 1
            p.wait("tensor", act, gubA[b0])
            p.wait("tensor", dve, gubD[b0])
            p.wait("tensor", act, gubA[b1])
            p.wait("tensor", dve, gubD[b1])
            for ck in range(CK):
                p.op("tensor", lambda t, ck=ck, wp=wp, b0=b0: t.matmul(
                    self.ps[b0][:, :], gu_sl[wp][:, ck * 256: ck * 256 + 128],
                    lnt(ck), start=(ck == 0), stop=(ck == CK - 1)),
                    inc=pe if ck == CK - 1 else None)
            for ck in range(CK):
                tok_u = p.op("tensor", lambda t, ck=ck, wp=wp, b1=b1: t.matmul(
                    self.ps[b1][:, :], gu_sl[wp][:, ck * 256 + 128: ck * 256 + 256],
                    lnt(ck), start=(ck == 0), stop=(ck == CK - 1)),
                    inc=pe if ck == CK - 1 else None)
            gu_free[wp] = tok_u
            ga = gta[gi % 2]
            p.wait("scalar", pe, tok_u)
            p.wait("scalar", dve, gta_free[gi % 2])
            tok_si = p.op("scalar", lambda s, b0=b0, ga=ga: s.activation(
                ga, self.ps[b0][:, :], AF.Silu), inc=act)
            gubA[b0] = tok_si
            p.wait("vector", act, tok_si)
            if cbe is None:
                tok_h = p.op("vector", lambda v, dst_slot=dst_slot, b1=b1,
                             ga=ga: v.tensor_tensor(
                    dst_slot, ga, self.ps[b1][:, :], op=ALU.mult), inc=dve)
            else:
                p.op("vector", lambda v, b1=b1, ga=ga: v.tensor_tensor(
                    ga, ga, self.ps[b1][:, :], op=ALU.mult), inc=dve)
                tok_h = p.op("vector", lambda v, dst_slot=dst_slot, cbe=cbe,
                             ga=ga: v.tensor_tensor(
                    dst_slot, ga, cbe, op=ALU.mult), inc=dve)
            gubD[b1] = tok_h
            gta_free[gi % 2] = tok_h
            return tok_h

        def comb_tile(tt):
            tc = slice(tt * 512, (tt + 1) * 512)
            lnt = lambda ck, tt=tt: lnb[:, ck * TC + tt * 512:
                                        ck * TC + (tt + 1) * 512]
            for ck in range(CK):
                tok_sc = p.op("tensor", lambda t, ck=ck: t.matmul(
                    self.ps[4][0:8, :], gw_sb[:, ck * 8:(ck + 1) * 8], lnt(ck),
                    start=(ck == 0), stop=(ck == CK - 1)),
                    inc=pe if ck == CK - 1 else None)
            p.wait("vector", pe, tok_sc)
            tok_em = p.op("vector", lambda v: v.tensor_copy(sc_em, self.ps[4][0:8, :]),
                          inc=dve)
            p.wait("tensor", dve, tok_em)
            for su in range(4):
                tok_tr = p.op("tensor", lambda t, su=su: t.transpose(
                    self.ps[5][:, 0:8], sc_em[:, su * 128:(su + 1) * 128],
                    self.identf[0:8, 0:8]), inc=pe)
                p.wait("vector", pe, tok_tr)
                tok_ev = p.op("vector", lambda v, su=su: v.tensor_copy(
                    sc_tm[:, su * 8:(su + 1) * 8], self.ps[5][:, 0:8]), inc=dve)
                p.wait("tensor", dve, tok_ev)
            for su in range(4):
                s8 = sc_tm[:, su * 8:(su + 1) * 8]
                c8 = cb_tm[:, su * 8:(su + 1) * 8]
                mk = msk[:, su * 8:(su + 1) * 8]
                p.op("vector", lambda v, s8=s8: v.tensor_reduce(
                    m1c, s8, axis=AX, op=ALU.max), inc=dve)
                p.op("vector", lambda v, s8=s8, mk=mk: v.tensor_scalar(
                    mk, s8, m1c, None, op0=ALU.is_ge), inc=dve)
                p.op("vector", lambda v, s8=s8, mk=mk, c8=c8: v.scalar_tensor_tensor(
                    c8, mk, -1e30, s8, op0=ALU.mult, op1=ALU.add), inc=dve)
                p.op("vector", lambda v, c8=c8: v.tensor_reduce(
                    m2c, c8, axis=AX, op=ALU.max), inc=dve)
                tok_d = p.op("vector", lambda v: v.tensor_tensor(
                    w1c, m1c, m2c, op=ALU.subtract), inc=dve)
                p.wait("scalar", dve, tok_d)
                p.op("scalar", lambda s: s.activation(w1c, w1c, AF.Sigmoid), inc=act)
                tok_w2 = p.op("scalar", lambda s: s.activation(
                    w2c, w1c, AF.Copy, scale=self.constf(-1.0), bias=1.0), inc=act)
                p.wait("vector", act, tok_w2)
                p.op("vector", lambda v, c8=c8: v.tensor_scalar(
                    c8, c8, m2c, None, op0=ALU.is_ge), inc=dve)
                p.op("vector", lambda v, c8=c8: v.tensor_scalar(
                    c8, c8, w2c, None, op0=ALU.mult), inc=dve)
                p.op("vector", lambda v, mk=mk: v.tensor_scalar(
                    mk, mk, w1c, None, op0=ALU.mult), inc=dve)
                tok_cb = p.op("vector", lambda v, c8=c8, mk=mk: v.tensor_tensor(
                    c8, c8, mk, op=ALU.add), inc=dve)
            for e in range(8):
                p.wait("tensor", dve, tok_cb)
                for su in range(4):
                    tok_cr = p.op("tensor", lambda t, e=e, su=su: t.matmul(
                        self.ps[4][0:1, :], cb_tm[:, su * 8 + e: su * 8 + e + 1],
                        ish_sb[:, 384 - 128 * su: 896 - 128 * su],
                        start=(su == 0), stop=(su == 3)), inc=pe)
                p.wait("vector", pe, tok_cr)
                tok_ra = p.op("vector", lambda v: v.tensor_copy(
                    self.rowa[:], self.ps[4][0:1, :]), inc=dve)
                p.wait("tensor", dve, tok_ra)
                tok_bc = p.op("tensor", lambda t: t.matmul(
                    self.ps[5][:, :], self.onesrow_f[0:1, 0:128], self.rowa[:],
                    start=True, stop=True), inc=pe)
                p.wait("vector", pe, tok_bc)
                tok_cbe = p.op("vector", lambda v, e=e: v.tensor_copy(
                    cbc[e], self.ps[5][:, :]), inc=dve)
                p.wait("tensor", dve, tok_cbe)
            for ck in range(CK):
                tok_se = p.op("tensor", lambda t, ck=ck: t.matmul(
                    self.ps[4][0:1, :], seg_sb[:, ck:ck + 1], lnt(ck),
                    start=(ck == 0), stop=(ck == CK - 1)),
                    inc=pe if ck == CK - 1 else None)
            p.wait("scalar", pe, tok_se)
            tok_sg = p.op("scalar", lambda s: s.activation(
                self.rowa[:], self.ps[4][0:1, :], AF.Sigmoid), inc=act)
            p.wait("tensor", act, tok_sg)
            tok_bc = p.op("tensor", lambda t: t.matmul(
                self.ps[5][:, :], self.onesrow_f[0:1, 0:128], self.rowa[:],
                start=True, stop=True), inc=pe)
            p.wait("vector", pe, tok_bc)
            tok_sgb = p.op("vector", lambda v: v.tensor_copy(sgb, self.ps[5][:, :]),
                           inc=dve)
            p.wait("tensor", dve, tok_sgb)

        def do_tile(tt):
            tc = slice(tt * 512, (tt + 1) * 512)
            lnt = lambda ck, tt=tt: lnb[:, ck * TC + tt * 512:
                                        ck * TC + (tt + 1) * 512]
            for hk in range(CK):
                tok_hlast = gate_up(self.big1[:, hk * 512:(hk + 1) * 512],
                                    shgu_r[:, :, hk * 256:(hk + 1) * 256], hk,
                                    None, lnt)
            p.wait("tensor", dve, tok_hlast)
            for dc in range(CK):
                wp = dc % 2
                p.wait("sync", pe, wd_free[wp])
                tok_w = p.dma("sync",
                              wd_sl[wp].rearrange("p (hk d) -> p hk d", hk=8),
                              shd_r[:, 0:8, dc * 128:(dc + 1) * 128], qd6[wp])
                p.wait("sync", pe, gu_free[wp])
                tok_w2 = p.dma("sync",
                               gu_sl[wp][:, 0:1024].rearrange("p (hk d) -> p hk d", hk=8),
                               shd_r[:, 8:16, dc * 128:(dc + 1) * 128], qd6[wp])
                p.wait("tensor", qd6[wp], tok_w2)
                bank = 2 + dc % 2
                p.wait("tensor", dve, dbank_free[dc % 2])
                p.wait("tensor", act, gubA[bank])
                p.wait("tensor", dve, gubD[bank])
                for hk in range(8):
                    p.op("tensor", lambda t, hk=hk, wp=wp, bank=bank: t.matmul(
                        self.ps[bank][:, :], wd_sl[wp][:, hk * 128:(hk + 1) * 128],
                        self.big1[:, hk * 512:(hk + 1) * 512],
                        start=(hk == 0), stop=False), inc=None)
                for hk in range(8):
                    tok_d = p.op("tensor", lambda t, hk=hk, wp=wp, bank=bank: t.matmul(
                        self.ps[bank][:, :],
                        gu_sl[wp][:, hk * 128:(hk + 1) * 128],
                        self.big1[:, (8 + hk) * 512:(9 + hk) * 512],
                        start=False, stop=(hk == 7)), inc=pe)
                wd_free[wp] = tok_d
                gu_free[wp] = max(gu_free[wp], tok_d)
                xp = dc % 2
                p.wait("sync", dve, x_free[xp])
                tok_x = p.dma("sync", xstr[xp], src_r[:, dc, tc], q6x[xp])
                p.wait("vector", pe, tok_d)
                p.wait("vector", q6x[xp], tok_x)
                tok_sh = p.op("vector", lambda v, bank=bank: v.tensor_tensor(
                    self.tb[:], self.ps[bank][:, :], sgb, op=ALU.mult), inc=dve)
                dbank_free[dc % 2] = tok_sh
                gubD[bank] = tok_sh
                tok_o = p.op("vector", lambda v, xp=xp: v.tensor_tensor(
                    xtmp[:], self.tb[:], xstr[xp], op=ALU.add), inc=dve)
                x_free[xp] = tok_o
                p.wait("scalar", dve, tok_o)
                p.dma("scalar", dst_r[:, dc, tc], xtmp[:], qst)
                p.wait_all("vector", qst)
            for e in range(8):
                for hk in range(8):
                    tok_hlast = gate_up(
                        self.big1[:, (e * 8 + hk) * 512:(e * 8 + hk + 1) * 512],
                        wguC_r[e, :, :, hk * 256:(hk + 1) * 256],
                        e * 8 + hk, cbc[e], lnt)
            if tt + 1 < 4:
                comb_tile(tt + 1)
            p.wait("tensor", dve, tok_hlast)
            for dc in range(CK):
                bank = 2 + dc % 2
                p.wait("tensor", dve, dbank_free[dc % 2])
                p.wait("tensor", act, gubA[bank])
                p.wait("tensor", dve, gubD[bank])
                tok_d = 0
                for e in range(8):
                    wp = (dc * 8 + e) % 2
                    p.wait("sync", pe, wd_free[wp])
                    tok_w = p.dma("sync",
                                  wd_sl[wp].rearrange("p (hk d) -> p hk d", hk=8),
                                  wdC_r[e, :, :, dc * 128:(dc + 1) * 128], qd6[wp])
                    p.wait("tensor", qd6[wp], tok_w)
                    for hk in range(8):
                        tok_d = p.op("tensor", lambda t, e=e, hk=hk, wp=wp,
                                     bank=bank: t.matmul(
                            self.ps[bank][:, :], wd_sl[wp][:, hk * 128:(hk + 1) * 128],
                            self.big1[:, (e * 8 + hk) * 512:(e * 8 + hk + 1) * 512],
                            start=(e == 0 and hk == 0), stop=(e == 7 and hk == 7)),
                            inc=pe)
                    wd_free[wp] = tok_d
                xp = dc % 2
                p.wait("sync", dve, x_free[xp])
                tok_x = p.dma("sync", xstr[xp], dst_r[:, dc, tc], q6x[xp])
                p.wait("vector", pe, tok_d)
                p.wait("vector", q6x[xp], tok_x)
                tok_o = p.op("vector", lambda v, xp=xp, bank=bank: v.tensor_tensor(
                    xtmp[:], self.ps[bank][:, :], xstr[xp], op=ALU.add), inc=dve)
                x_free[xp] = tok_o
                dbank_free[dc % 2] = tok_o
                gubD[bank] = tok_o
                p.wait("scalar", dve, tok_o)
                p.dma("scalar", dst_r[:, dc, tc], xtmp[:], qst)
                p.wait_all("vector", qst)

        comb_tile(0)
        for tt in range(4):
            do_tile(tt)
        p.wait_all("sync", qst)
    def p7(self):
        p, nc = self.p, self.nc
        pe, dve, act, qld, qst, qgp = (self.pe, self.dve, self.act, self.qld,
                                       self.qst, self.qgp)
        lnb = self.big0
        lnt = lambda ck, tt: lnb[:, ck * TC + tt * 512: ck * TC + (tt + 1) * 512]
        w_sl = [self.lns[:, i * 2048:(i + 1) * 2048] for i in range(2)]
        kv_sl = [self.big1[:, i * 8192:(i + 1) * 8192] for i in range(2)]
        vk_sb = self.xs[:, 5440:7488]
        vkT_sb = self.xs[:, 2592:3648].bitcast(BF16)
        hbK = self.lns[:, 4096:4624]
        hbV = self.lns[:, 5152:5680]
        qhb = self.lns[:, 6144:8192]
        xstr = [self.xs[:, 3840 + i * 512: 3840 + (i + 1) * 512] for i in range(2)]
        xtmp = self.xs[:, 4864:5376]
        bpr = self.xs[:, 5376:5392]
        q7 = [p.sem("q7a"), p.sem("q7b")]
        q7k = [p.sem("q7ka"), p.sem("q7kb")]
        q7w = [p.sem("q7wa"), p.sem("q7wb")]
        q7m = p.sem("q7m")
        ccr = p.sem("ccr")
        x2_r = self.x2r.rearrange("(ck p) t -> p ck t", p=128)
        x3_r = self.x3r.rearrange("(ck p) t -> p ck t", p=128)
        wq_r = self.wqkvT.rearrange("(ck p) d -> p ck d", p=128)
        wp_r = self.wprojT.rearrange("(ck p) d -> p ck d", p=128)
        resq = {}
        qbp = p.sem("qbp")
        tok_bp = p.dma("sync", bpr, self.bprojR, qbp)

        wfree = [0, 0]
        for dc in range(CK):
            wp = dc % 2
            p.wait("gpsimd", pe, wfree[wp])
            tok_w = p.dma("gpsimd", w_sl[wp].rearrange("p (ck d) -> p ck d", ck=CK),
                          wq_r[:, :, dc * 128:(dc + 1) * 128], q7w[wp])
            p.wait("tensor", q7w[wp], tok_w)
            for tt in range(4):
                bank = tt % 2
                p.wait("tensor", act, resq.get(bank, 0))
                for ck in range(CK):
                    tok_m = p.op("tensor", lambda t, ck=ck, tt=tt, wp=wp, bank=bank: t.matmul(
                        self.ps[bank][:, :], w_sl[wp][:, ck * 128:(ck + 1) * 128],
                        lnt(ck, tt), start=(ck == 0), stop=(ck == CK - 1)),
                        inc=pe if ck == CK - 1 else None)
                p.wait("scalar", pe, tok_m)
                p.wait_all("scalar", qst)
                tok_e = p.op("scalar", lambda s, bank=bank: s.activation(
                    self.ta[:], self.ps[bank][:, :], AF.Relu), inc=act)
                resq[bank] = tok_e
                p.wait("scalar", act, tok_e)
                p.dma("scalar", self.qb_fm[dc * 128:(dc + 1) * 128,
                                           tt * 512:(tt + 1) * 512], self.ta[:], qst)
            wfree[wp] = pe.n

        kvfree = [0, 0]
        gi = 0
        for which in range(2):
            dstd = (self.kb_tm, self.vb_tm)[which]
            relu = (True, False)[which]
            base = 2048 + which * 2048
            for g in range(4):
                sp = gi % 2
                gi += 1
                p.wait("gpsimd", pe, kvfree[sp])
                tok_w = p.dma("gpsimd",
                              kv_sl[sp].rearrange("p (ck d) -> p ck d", ck=CK),
                              wq_r[:, :, base + g * 512: base + (g + 1) * 512],
                              q7k[sp])
                p.wait("tensor", q7k[sp], tok_w)
                for tsub in range(16):
                    bank = tsub % 2
                    p.wait("tensor", act, resq.get(bank, 0))
                    for ck in range(CK):
                        tok_m = p.op("tensor", lambda t, ck=ck, g=g, tsub=tsub,
                                     sp=sp, bank=bank: t.matmul(
                            self.ps[bank][:, :],
                            lnb[:, ck * TC + tsub * 128: ck * TC + (tsub + 1) * 128],
                            kv_sl[sp][:, ck * 512:(ck + 1) * 512],
                            start=(ck == 0), stop=(ck == CK - 1)),
                            inc=pe if ck == CK - 1 else None)
                    p.wait("scalar", pe, tok_m)
                    p.wait_all("scalar", qst)
                    tok_e = p.op("scalar", lambda s, bank=bank, relu=relu: s.activation(
                        self.ta[:], self.ps[bank][:, :],
                        AF.Relu if relu else AF.Copy), inc=act)
                    resq[bank] = tok_e
                    p.wait("scalar", act, tok_e)
                    p.dma("scalar", dstd[tsub * 128:(tsub + 1) * 128,
                                         g * 512:(g + 1) * 512], self.ta[:], qst)
                kvfree[sp] = pe.n
        p.wait_all("sync", qst)

        hb3K = hbK.rearrange("p (n e) -> p n e", e=33)
        hb3V = hbV.rearrange("p (n e) -> p n e", e=33)
        tok_on = p.op("vector", lambda v: v.memset(hb3V[:, :, 32:33], 1.0))
        p.wait("sync", dve, tok_on)
        hfree = 0
        for h in range(64):
            p.wait("sync", pe, hfree)
            p.dma("sync", hb3K[:, :, 0:32],
                  self.kb_tm[:, h * 32:(h + 1) * 32]
                  .rearrange("(n p) d -> p n d", p=128), q7m)
            tok_kv = p.dma("sync", hb3V[:, :, 0:32],
                           self.vb_tm[:, h * 32:(h + 1) * 32]
                           .rearrange("(n p) d -> p n d", p=128), q7m)
            p.wait("tensor", q7m, tok_kv)
            for n_ in range(16):
                tok_m = p.op("tensor", lambda t, n_=n_: t.matmul(
                    self.ps[4][0:33, 0:32], hb3V[:, n_, :], hb3K[:, n_, 0:32],
                    start=(n_ == 0), stop=(n_ == 15)), inc=pe)
            hfree = tok_m
            p.wait("vector", pe, tok_m)
            tok_v = p.op("vector", lambda v, h=h: v.tensor_copy(
                vk_sb.rearrange("a (h2 d) -> a h2 d", d=32)[0:33, h, :]
                if False else vk_sb[0:33, h * 32:(h + 1) * 32],
                self.ps[4][0:33, 0:32]), inc=dve)
            p.wait("tensor", dve, tok_v)
        p.wait("scalar", dve, dve.n)
        tok_st = p.dma("scalar", self.vkp[:, :], vk_sb[0:33, :], qst)
        p.wait_all("gpsimd", qst)
        tok_cc = p.op("gpsimd", lambda g: g.collective_compute(
            "AllReduce", ALU.add,
            replica_groups=[[0, 1], [2, 3], [4, 5], [6, 7]],
            ins=[self.vkp], outs=[self.vkr]), inc=ccr)
        p.wait("sync", ccr, tok_cc)
        q7v = p.sem("q7v")
        tok_ld = p.dma("sync", vk_sb[0:33, :], self.vkr[:, :], q7v)
        p.wait("vector", q7v, tok_ld)
        tok_cast = p.op("vector", lambda v: v.tensor_copy(
            qhb[0:33, 0:2048], vk_sb[0:33, :]), inc=dve)
        p.wait("tensor", dve, tok_cast)
        vkbf = qhb
        tfree = 0
        for h in range(64):
            tok_tr = p.op("tensor", lambda t, h=h: t.transpose(
                self.ps[5][0:32, 0:33], vk_sb[0:33, h * 32:(h + 1) * 32],
                self.identf[0:33, 0:33]), inc=pe)
            p.wait("vector", pe, tok_tr)
            tok_ev = p.op("vector", lambda v, h=h: v.tensor_copy(
                vkT_sb[0:32, h * 33:(h + 1) * 33], self.ps[5][0:32, 0:33]), inc=dve)
            p.wait("tensor", dve, tok_ev)

        qfree = pe.n
        for h in range(64):
            p.wait("sync", pe, qfree)
            tok_q = p.dma("sync", qhb[0:32, :],
                          self.qb_fm[h * 32:(h + 1) * 32, :], q7m)
            p.wait("tensor", q7m, tok_q)
            for tt in range(4):
                p.wait("tensor", act, resq.get("onum", 0))
                tok_m = p.op("tensor", lambda t, h=h, tt=tt: t.matmul(
                    self.ps[4][0:33, :], vkT_sb[0:32, h * 33:(h + 1) * 33],
                    qhb[0:32, tt * 512:(tt + 1) * 512],
                    start=True, stop=True), inc=pe)
                qfree = tok_m
                p.wait("vector", pe, tok_m)
                tok_r = p.op("vector", lambda v: v.reciprocal(
                    self.rowa[:], self.ps[4][32:33, :]), inc=dve)
                p.wait("tensor", dve, tok_r)
                p.wait("tensor", dve, resq.get("odiv", 0))
                tok_b = p.op("tensor", lambda t: t.matmul(
                    self.ps[5][0:32, :], self.onesrow_f[0:1, 0:32], self.rowa[:],
                    start=True, stop=True), inc=pe)
                p.wait("scalar", pe, tok_b)
                tok_n = p.op("scalar", lambda s: s.activation(
                    self.ta[0:32, :], self.ps[4][0:32, :], AF.Copy), inc=act)
                resq["onum"] = tok_n
                p.wait("vector", act, tok_n)
                p.wait_all("vector", qst)
                tok_o = p.op("vector", lambda v: v.tensor_tensor(
                    self.tb[0:32, :], self.ta[0:32, :], self.ps[5][0:32, :],
                    op=ALU.mult), inc=dve)
                resq["odiv"] = tok_o
                p.wait("scalar", dve, tok_o)
                p.dma("scalar", self.abo[h * 32:(h + 1) * 32,
                                         tt * 512:(tt + 1) * 512],
                      self.tb[0:32, :], qst)
        p.wait_all("sync", qst)

        abo_res = self.big1[:, 0:32768]
        ab3 = abo_res.rearrange("p (ck t) -> p ck t", ck=CK)
        tok_ab = p.dma("sync", ab3,
                       self.abo.rearrange("(ck p) t -> p ck t", p=128), q7m)
        p.wait("tensor", q7m, tok_ab)
        p.wait("vector", qbp, tok_bp)
        xfree = [0, 0]
        bfree = [0, 0]
        for dc in range(CK):
            wp = dc % 2
            p.wait("gpsimd", pe, wfree[wp])
            tok_w = p.dma("gpsimd", w_sl[wp].rearrange("p (ck d) -> p ck d", ck=CK),
                          wp_r[:, :, dc * 128:(dc + 1) * 128], q7w[wp])
            p.wait("tensor", q7w[wp], tok_w)
            for tt in range(4):
                bank = tt % 2
                p.wait("tensor", dve, bfree[bank])
                p.wait("tensor", act, resq.get(bank, 0))
                for ck in range(CK):
                    tok_m = p.op("tensor", lambda t, ck=ck, tt=tt, wp=wp, bank=bank: t.matmul(
                        self.ps[bank][:, :], w_sl[wp][:, ck * 128:(ck + 1) * 128],
                        ab3[:, ck, tt * 512:(tt + 1) * 512],
                        start=(ck == 0), stop=(ck == CK - 1)),
                        inc=pe if ck == CK - 1 else None)
                xp = tt % 2
                p.wait("sync", dve, xfree[xp])
                tok_x = p.dma("sync", xstr[xp], x2_r[:, dc, tt * 512:(tt + 1) * 512],
                              q7[xp])
                p.wait("vector", pe, tok_m)
                p.wait("vector", q7[xp], tok_x)
                p.op("vector", lambda v, bank=bank, dc=dc: v.tensor_scalar(
                    xtmp[:], self.ps[bank][:, :], bpr[:, dc:dc + 1], None,
                    op0=ALU.add), inc=dve)
                tok_o = p.op("vector", lambda v, xp=xp: v.tensor_tensor(
                    xtmp[:], xtmp[:], xstr[xp], op=ALU.add), inc=dve)
                xfree[xp] = tok_o
                bfree[bank] = tok_o
                p.wait("scalar", dve, tok_o)
                p.dma("scalar", x3_r[:, dc, tt * 512:(tt + 1) * 512], xtmp[:], qst)
                p.wait_all("vector", qst)
            wfree[wp] = pe.n
        p.wait_all("sync", qst)
    def finish(self):
        p = self.p
        p.wait("gpsimd", self.qst, self.qst.n)
        p.wait("gpsimd", self.qld, self.qld.n)
        p.wait("gpsimd", self.qgp, self.qgp.n)


def build_nc(upto="all", dbg=()):
    nc = bass.Bass("TRN2", target_bir_lowering=False)
    with ExitStack() as ctx:
        k = K(nc, ctx, dbg)
        k.p0()
        k.p0b()
        if upto in ("p1", "p2", "p3", "p5", "p6", "p7", "all"):
            k.p1()
        if upto in ("p2", "p3", "p5", "p6", "p7", "all"):
            k.phase_sync("p1")
            k.cast_wait()
            if upto == "p2":
                k.res23 = {}
                for _ in k.p2():
                    pass
            else:
                k.p23()
        if upto in ("p5", "p6", "p7", "all"):
            k.p4()
            k.phase_sync("p4")
            k.p5()
        if upto in ("p6", "p7", "all"):
            k.phase_sync("p5")
            k.ffn(k.x1r, k.x2r, "f1")
        if upto in ("p7", "all"):
            k.phase_sync("p6")
            k.ln_stream(k.x2r, k.big0,
                        dbg_out=k.ln_d if "ln_d" in k.dbg else None)
            k.p7()
        if upto == "all":
            k.phase_sync("p7")
            k.ln_stream(k.x3r, k.big0)
            k.phase_sync("ln4")
            k.ffn(k.x3r, k.out_fm, "f2")
        if upto == "all":
            pass
        k.finish()
        block = ctx.enter_context(nc.Block())
        k.p.emit(block)
    return nc


def kernel(**inputs):
    in_maps = stage(inputs)
    nc = build_nc()
    res = run_bass_kernel_spmd(nc, in_maps, core_ids=list(range(8)))
    x = np.asarray(inputs["L0_attn_a_hidden_states"])
    out = np.empty_like(x)
    for core in range(8):
        b_, par = core // 2, core % 2
        out[b_, par * TC:(par + 1) * TC, :] = res.results[core]["out_fm"].T
    return out


# version 11
# speedup vs baseline: 1.1920x; 1.1920x over previous
import numpy as np
from contextlib import ExitStack

import concourse.bass as bass
import concourse.mybir as mybir
from concourse.alu_op_type import AluOpType as ALU
from concourse.bass_utils import run_bass_kernel_spmd

F32 = mybir.dt.float32
BF16 = mybir.dt.bfloat16
AF = mybir.ActivationFunctionType
AX = mybir.AxisListType.X

B, N, C = 4, 4096, 2048
H, HKV, D = 16, 8, 128
HQ, KV = 8, 4
TC = 2048
SCALE = float(1.0 / np.sqrt(128.0))
NSL, SL = 8, 512
CK = 16
CH, NCH = 128, 32
FLA_W = 0.01

ENGINES = ("sync", "gpsimd", "tensor", "vector", "scalar")


class Sem:
    def __init__(self, h):
        self.h = h
        self.n = 0


class P:

    def __init__(self, nc, ctx):
        self.nc = nc
        self.ctx = ctx
        self.streams = {e: [] for e in ENGINES}
        self._sems = {}
        self.self_sems = {}

    def sem(self, name):
        if name not in self._sems:
            self._sems[name] = Sem(self.ctx.enter_context(self.nc.semaphore(name)))
        return self._sems[name]

    def op(self, engine, f, inc=None, inck=1, raw=False):
        if not raw and engine in ("vector", "scalar"):
            own = self.self_sems.get(engine)
            if own is None:
                own = self.sem("self_" + engine)
                self.self_sems[engine] = own
            if inc is None:
                inc = own
            assert inc is own, "vector/scalar ops must inc their own sem"
            self.wait(engine, own, own.n)
        tok = None
        h = None
        if inc is not None:
            inc.n += inck
            tok = inc.n
            h = inc.h

        def run(eng, f=f, h=h, inck=inck):
            ins = f(eng)
            if h is not None:
                ins.then_inc(h, inck)
        self.streams[engine].append(run)
        return tok

    def wait(self, engine, sem, val):
        if val is None or val <= 0:
            return
        self.streams[engine].append(lambda eng, s=sem.h, v=val: eng.wait_ge(s, v))

    def wait_all(self, engine, sem):
        self.wait(engine, sem, sem.n)

    def dma(self, engine, out, in_, sem, **kw):
        return self.op(engine,
                       lambda e, o=out, i=in_, kw=kw: e.dma_start(out=o, in_=i, **kw),
                       inc=sem, inck=16, raw=True)

    def barrier(self, name):
        s = self.sem("bar_" + name)
        for e in ENGINES:
            self.op(e, lambda eng, s=s: eng.sem_inc(s.h, 1), raw=True)
            s.n += 1
        v = s.n
        for e in ENGINES:
            self.wait(e, s, v)

    def emit(self, block):
        for e in ENGINES:
            fns = self.streams[e]
            getattr(block, e)(lambda eng, fns=fns: [f(eng) for f in fns])



def stage(inp):
    import ml_dtypes
    c = np.ascontiguousarray
    x = np.asarray(inp["L0_attn_a_hidden_states"])
    wq = np.asarray(inp["L0_attn_a_Wq"])
    wk = np.asarray(inp["L0_attn_a_Wk"])
    wv = np.asarray(inp["L0_attn_a_Wv"])

    shared = {
        "wo": c(np.asarray(inp["L0_attn_a_Wo"])),
        "wg": c(np.asarray(inp["L0_attn_a_Wg"])),
        "bg": c(np.asarray(inp["L0_attn_a_bg"]).reshape(1, 1)),
        "wqkvT": c(np.asarray(inp["L0_attn_b_w_qkv"]).T),
        "wprojT": c(np.asarray(inp["L0_attn_b_w_proj"]).T),
        "bproj": c(np.asarray(inp["L0_attn_b_b_proj"]).reshape(C, 1)),
        "gwT": c(np.asarray(inp["L0_ffn_gate_w"]).T),
        "wguI": None,
        "shguI": None,
        "wdT": c(np.asarray(inp["L0_ffn_w_down"]).transpose(0, 2, 1)),
        "shdT": c(np.asarray(inp["L0_ffn_sh_down"]).T),
        "segT": c(np.asarray(inp["L0_ffn_se_gate_w"]).T),
        "segR": c(np.asarray(inp["L0_ffn_se_gate_w"]).reshape(16, 128).T),
        "bprojR": c(np.asarray(inp["L0_attn_b_b_proj"]).reshape(16, 128).T),
    }
    wgT = np.asarray(inp["L0_ffn_w_gate"]).transpose(0, 2, 1)
    wuT = np.asarray(inp["L0_ffn_w_up"]).transpose(0, 2, 1)
    shared["wguI"] = c(np.stack([wgT.reshape(8, C, 8, 128),
                                 wuT.reshape(8, C, 8, 128)], axis=3)
                       .reshape(8, C, 2048))
    sgT = np.asarray(inp["L0_ffn_sh_gate"]).T.reshape(C, 16, 128)
    suT = np.asarray(inp["L0_ffn_sh_up"]).T.reshape(C, 16, 128)
    shared["shguI"] = c(np.stack([sgT, suT], axis=2).reshape(C, 4096))
    ident = np.eye(128, dtype=np.float32)
    shared["ident_f"] = ident
    shared["ident_b"] = ident.astype(ml_dtypes.bfloat16)
    tl = np.tril(np.ones((128, 128), np.float32), -1)
    shared["mask_ls"] = tl
    shared["mask_us"] = c(tl.T)
    shared["mask_ui"] = c(np.triu(np.ones((128, 128), np.float32), 0))
    ones = np.ones((128, 512), np.float32)
    shared["ones_f"] = ones
    shared["ones_c"] = np.ones((128, 1), np.float32)
    shared["ones_b"] = ones.astype(ml_dtypes.bfloat16)
    cm = np.zeros((4, 128, 512), np.float32)
    kk, qq = np.meshgrid(np.arange(128), np.arange(512), indexing="ij")
    for s_ in range(4):
        cm[s_] = (128 * s_ + kk <= qq).astype(np.float32)
    shared["cmask"] = c(cm)
    ish = np.zeros((128, 1024), np.float32)
    ish[np.arange(128), 384 + np.arange(128)] = 1.0
    shared["ishift"] = ish

    in_maps = []
    xt = [c(x[b_].T) for b_ in range(B)]
    for core in range(8):
        b_, par = core // 2, core % 2
        m = dict(shared)
        m["x_fm"] = xt[b_]
        m["x_own"] = c(xt[b_][:, par * TC:(par + 1) * TC])
        m["wq_p"] = c(wq[:, par * 1024:(par + 1) * 1024])
        m["wk_p"] = c(wk[:, par * 512:(par + 1) * 512])
        m["wv_p"] = c(wv[:, par * 512:(par + 1) * 512])
        m["wkT_p"] = c(wk.T[par * 512:(par + 1) * 512, :])
        ps_ = np.zeros((128, 2), np.float32)
        ps_[:, par] = 1.0
        m["psel"] = ps_
        in_maps.append(m)
    return in_maps



class K:

    def __init__(self, nc, ctx, dbg):
        self.nc = nc
        self.ctx = ctx
        self.dbg = set(dbg)
        self.p = P(nc, ctx)

        din = lambda n, s, d=F32: nc.dram_tensor(n, s, d, kind="ExternalInput").ap()
        self.x_fm = din("x_fm", [C, N])
        self.x_own = din("x_own", [C, TC])
        self.wq_p = din("wq_p", [C, 1024])
        self.wk_p = din("wk_p", [C, 512])
        self.wv_p = din("wv_p", [C, 512])
        self.wkT_p = din("wkT_p", [512, C])
        self.wo = din("wo", [C, C])
        self.wg = din("wg", [128, 1])
        self.bg = din("bg", [1, 1])
        self.wqkvT = din("wqkvT", [C, 3 * C])
        self.wprojT = din("wprojT", [C, C])
        self.bproj = din("bproj", [C, 1])
        self.gwT = din("gwT", [C, 8])
        self.wguI = din("wguI", [8, C, 2048])
        self.wdT = din("wdT", [8, 1024, C])
        self.shguI = din("shguI", [C, 4096])
        self.shdT = din("shdT", [C, C])
        self.segT = din("segT", [C, 1])
        self.segR = din("segR", [128, 16])
        self.bprojR = din("bprojR", [128, 16])
        self.ident_f = din("ident_f", [128, 128])
        self.ident_b = din("ident_b", [128, 128], BF16)
        self.mask_ls = din("mask_ls", [128, 128])
        self.mask_us = din("mask_us", [128, 128])
        self.mask_ui = din("mask_ui", [128, 128])
        self.ones_f = din("ones_f", [128, 512])
        self.ones_c = din("ones_c", [128, 1])
        self.ones_b = din("ones_b", [128, 512], BF16)
        self.cmask = din("cmask", [4, 128, 512])
        self.ishift = din("ishift", [128, 1024])
        self.psel = din("psel", [128, 2])

        def scr(n, s, d):
            kind = "ExternalOutput" if n in self.dbg else "Internal"
            return nc.dram_tensor(n, s, d, kind=kind).ap()
        self.q_fm = scr("q_fm", [1024, N], BF16)
        self.qh_fm = scr("qh_fm", [1024, N], BF16)
        self.k_fm = scr("k_fm", [512, N], BF16)
        self.kh_fm = scr("kh_fm", [512, N], BF16)
        self.kh_tm = scr("kh_tm", [KV, N, 128], BF16)
        self.v_tm = scr("v_tm", [N, 512], BF16)
        self.gs_d = scr("gs_d", [KV, N], F32)
        self.contrib = scr("contrib", [2 * 1024, TC // 2], F32)
        self.contribW = scr("contribW", [2 * 1024, TC // 2], F32)
        self.gath = scr("gath", [4 * 1024, TC // 2], F32)
        self.gathW = scr("gathW", [4 * 1024, TC // 2], F32)
        self.x1r = scr("x1r", [C, TC], F32)
        self.x2r = scr("x2r", [C, TC], F32)
        self.x3r = scr("x3r", [C, TC], F32)
        self.ln_d = scr("ln_d", [C, TC], BF16)
        self.vkp = scr("vkp", [33, C], F32)
        self.vkr = scr("vkr", [33, C], F32)
        self.qb_fm = scr("qb_fm", [C, TC], BF16)
        self.kb_tm = scr("kb_tm", [TC, C], BF16)
        self.vb_tm = scr("vb_tm", [TC, C], BF16)
        self.abo = scr("abo", [C, TC], BF16)
        self.out_fm = nc.dram_tensor("out_fm", [C, TC], F32, kind="ExternalOutput").ap()
        self.wguC = scr("wguC", [8, C, 2048], BF16)
        self.wdC = scr("wdC", [8, 1024, C], BF16)
        self.shguC = scr("shguC", [C, 4096], BF16)
        self.shdC = scr("shdC", [C, C], BF16)

        s = lambda n, sh, d: ctx.enter_context(nc.sbuf_tensor(n, sh, d))
        self.xs = s("xs", [128, CK * SL], F32)
        self.lns = s("lns", [128, CK * SL], BF16)
        self.big0 = s("big0", [128, 32768], BF16)
        self.big1 = s("big1", [128, 32768], BF16)
        self.wq_sb = self.big0[:, 0:16384]
        self.wk_sb = self.big0[:, 16384:24576]
        self.wv_sb = self.big0[:, 24576:32768]
        self.wg2 = s("wg2", [128, 64], F32)
        self.wg2b = s("wg2b", [128, 64], BF16)
        self.wgv = s("wgv", [128, 1], F32)
        self.bgv = s("bgv", [1, 1], F32)
        self.wkT_sb = self.xs[:, 0:2048]
        self.eq = s("eq", [128, 512], F32)
        self.sq = [s(f"sq{i}", [128, 512], F32) for i in range(2)]
        self.rowa = s("rowa", [1, 512], F32)
        self.rowb = s("rowb", [1, 512], F32)
        self.rowc = s("rowc", [1, 512], F32)
        self.gsst = s("gsst", [1, 512], F32)
        self.ta = s("ta", [128, 512], BF16)
        self.tb = s("tb", [128, 512], BF16)
        self.onescol_f = s("onescol_f", [128, 1], F32)
        self.cstp = s("cstp", [128, 16], F32)
        self._cst = {}
        self.onesrow_f = s("onesrow_f", [1, 512], F32)
        self.identb = s("identb", [128, 128], BF16)
        self.identf = s("identf", [128, 128], F32)

        self.ps = [ctx.enter_context(nc.psum_tensor(f"ps{i}", [128, 512], F32))
                   for i in range(6)]
        self.psT = [ctx.enter_context(nc.psum_tensor(f"psT{i}", [128, 512], BF16))
                    for i in range(2)]

        p = self.p
        self.qld = p.sem("qld")
        self.cini = p.sem("cini")
        self.res_gsst = 0
        for v_ in (1e-5, -0.5, -1.0, 1.0 / 16.0, 2.0, float(SCALE * FLA_W),
                   0.99, 0.01, float(SCALE), 0.5, -2.0):
            self.constf(v_)
        self.qst = p.sem("qst")
        self.qgp = p.sem("qgp")
        self.pe = p.sem("pe")
        self.dve = p.sem("self_vector")
        self.act = p.sem("self_scalar")
        p.self_sems["vector"] = self.dve
        p.self_sems["scalar"] = self.act

    def constf(self, val):
        val = float(val)
        if val not in self._cst:
            i = len(self._cst)
            assert i < 16
            self._cst[val] = i
            self.nc.const_aps.register(F32, val, self.cstp[:, i:i + 1])
        return val

    def phase_sync(self, name):
        p = self.p
        for e in ENGINES:
            for s in (self.qld, self.qst, self.qgp):
                p.wait_all(e, s)
        p.barrier(name)

    def p0(self):
        p, nc = self.p, self.nc
        def memset_all(v):
            ins = v.memset(self.cstp[:, 15:16], 0.0)
            for val, i in self._cst.items():
                ins = v.memset(self.cstp[:, i:i + 1], val)
            return ins
        p.op("vector", memset_all, inc=self.dve)
        t_c = self.dve.n
        for e in ("scalar", "tensor", "gpsimd"):
            p.wait(e, self.dve, t_c)
        p.dma("sync", self.onescol_f[:], self.ones_c, self.qld)
        p.dma("sync", self.onesrow_f[:], self.ones_f[0:1, :], self.qld)
        p.dma("sync", self.identb[:], self.ident_b, self.qld)
        p.dma("sync", self.identf[:], self.ident_f, self.qld)
        p.dma("sync", self.wgv[:], self.wg, self.qld)
        p.dma("sync", self.bgv[:], self.bg, self.qld)
        p.dma("gpsimd", self.wq_sb.rearrange("p (ck q) -> p ck q", ck=CK),
              self.wq_p.rearrange("(ck p) q -> p ck q", p=128), self.qgp)
        p.dma("gpsimd", self.wk_sb.rearrange("p (ck q) -> p ck q", ck=CK),
              self.wk_p.rearrange("(ck p) q -> p ck q", p=128), self.qgp)
        p.dma("gpsimd", self.wv_sb.rearrange("p (ck q) -> p ck q", ck=CK),
              self.wv_p.rearrange("(ck p) q -> p ck q", p=128), self.qgp)
        t_ld = self.qld.n

        for j in range(KV):
            p.wait("sync", self.pe, self.pe.n)
            tok = p.dma("sync", self.wkT_sb, self.wkT_p[j * 128:(j + 1) * 128, :],
                        self.qld)
            p.wait("tensor", self.qld, tok)
            for ck in range(CK):
                tok_pe = p.op("tensor", lambda t, ck=ck: t.matmul(
                    self.ps[0][:, 0:1], self.wkT_sb[:, ck * 128:(ck + 1) * 128],
                    self.wgv[:], start=True, stop=True), inc=self.pe)
                p.wait("vector", self.pe, tok_pe)
                tok_dv = p.op("vector", lambda v, ck=ck, j=j: v.tensor_copy(
                    self.wg2[:, ck * 4 + j:ck * 4 + j + 1], self.ps[0][:, 0:1]),
                    inc=self.dve)
                p.wait("tensor", self.dve, tok_dv)
        tok_g2b = p.op("vector", lambda v: v.tensor_copy(self.wg2b[:], self.wg2[:]),
                       inc=self.dve)
        p.wait("tensor", self.dve, tok_g2b)

    def _sm_exp(self, bank, dst_raw, eqb):
        p = self.p
        tok_raw = p.op("scalar", lambda s: s.activation(
            self.ta[:], self.ps[bank][:, :], AF.Copy), inc=self.act)
        p.wait("scalar", self.act, tok_raw)
        p.dma("scalar", dst_raw, self.ta[:], self.qst)
        tok_eq = p.op("scalar", lambda s, eqb=eqb: s.activation(
            eqb, self.ps[bank][:, :], AF.Exp), inc=self.act)
        return tok_eq

    def _sm_ladder(self, tok_eq, eqb, dst_sm):
        p = self.p
        p.wait("tensor", self.act, tok_eq)
        tok_sum = p.op("tensor", lambda t, eqb=eqb: t.matmul(
            self.ps[2][0:1, :], self.onescol_f[:], eqb,
            start=True, stop=True), inc=self.pe)
        p.wait("vector", self.pe, tok_sum)
        tok_rc = p.op("vector", lambda v: v.reciprocal(self.rowa[:], self.ps[2][0:1, :]),
                      inc=self.dve)
        p.wait("tensor", self.dve, tok_rc)
        tok_bc = p.op("tensor", lambda t: t.matmul(
            self.ps[3][:, :], self.onesrow_f[0:1, 0:128], self.rowa[:],
            start=True, stop=True), inc=self.pe)
        p.wait("vector", self.pe, tok_bc)
        p.wait_all("vector", self.qst)
        tok_sm = p.op("vector", lambda v, eqb=eqb: v.tensor_tensor(
            self.tb[:], eqb, self.ps[3][:, :], op=ALU.mult), inc=self.dve)
        p.wait("scalar", self.dve, tok_sm)
        p.dma("scalar", dst_sm, self.tb[:], self.qst)
        return tok_sm

    def p1(self):
        p, nc = self.p, self.nc
        p.wait("tensor", self.qgp, self.qgp.n)
        p.wait("sync", self.pe, self.pe.n)
        x_r = self.x_fm.rearrange("(ck p) t -> p ck t", p=128)
        prev = {"xs_dve": 0, "xs_act": 0, "lns_pe": 0}
        for ts in range(NSL):
            cols = slice(ts * SL, (ts + 1) * SL)
            p.wait("sync", self.dve, prev["xs_dve"])
            p.wait("sync", self.act, prev["xs_act"])
            tok_x = p.dma("sync", self.xs[:].rearrange("p (ck t) -> p ck t", ck=CK),
                          x_r[:, :, cols], self.qld)
            p.wait("tensor", self.qld, tok_x)
            p.wait("scalar", self.qld, tok_x)
            for ck in range(CK):
                p.op("tensor", lambda t, ck=ck: t.matmul(
                    self.ps[2][0:1, :], self.onescol_f[:],
                    self.xs[:, ck * SL:(ck + 1) * SL],
                    start=(ck == 0), stop=(ck == CK - 1)),
                    inc=self.pe if ck == CK - 1 else None)
            sq_mm_tok = [0, 0]
            p.wait("scalar", self.dve, prev.get("sq_dve", 0))
            for ck in range(CK):
                sqb = self.sq[ck % 2]
                p.wait("scalar", self.pe, sq_mm_tok[ck % 2])
                tok_sq = p.op("scalar", lambda s, ck=ck, sqb=sqb: s.activation(
                    sqb[:], self.xs[:, ck * SL:(ck + 1) * SL], AF.Square),
                    inc=self.act)
                p.wait("tensor", self.act, tok_sq)
                tok_mm = p.op("tensor", lambda t, ck=ck, sqb=sqb: t.matmul(
                    self.ps[3][0:1, :], self.onescol_f[:], sqb[:],
                    start=(ck == 0), stop=(ck == CK - 1)), inc=self.pe)
                sq_mm_tok[ck % 2] = tok_mm
            t_stats = self.pe.n
            p.wait("vector", self.pe, t_stats)
            p.op("vector", lambda v: v.tensor_scalar(
                self.rowa[:], self.ps[2][0:1, :], 1.0 / C, None, op0=ALU.mult),
                inc=self.dve)
            p.op("vector", lambda v: v.tensor_scalar(
                self.rowb[:], self.ps[3][0:1, :], 1.0 / C, None, op0=ALU.mult),
                inc=self.dve)
            p.op("vector", lambda v: v.tensor_tensor(
                self.rowc[:], self.rowa[:], self.rowa[:], op=ALU.mult), inc=self.dve)
            tok_var = p.op("vector", lambda v: v.tensor_tensor(
                self.rowb[:], self.rowb[:], self.rowc[:], op=ALU.subtract), inc=self.dve)
            p.wait("scalar", self.dve, tok_var)
            p.op("scalar", lambda s: s.activation(self.rowb[:], self.rowb[:], AF.Ln,
                                                  bias=self.constf(1e-5)), inc=self.act)
            tok_rstd = p.op("scalar", lambda s: s.activation(
                self.rowb[:], self.rowb[:], AF.Exp, scale=self.constf(-0.5)), inc=self.act)
            p.wait("tensor", self.dve, tok_var)
            p.wait("tensor", self.act, tok_rstd)
            tok_mb = p.op("tensor", lambda t: t.matmul(
                self.ps[4][:, :], self.onesrow_f[0:1, 0:128], self.rowa[:],
                start=True, stop=True), inc=self.pe)
            tok_rb = p.op("tensor", lambda t: t.matmul(
                self.ps[5][:, :], self.onesrow_f[0:1, 0:128], self.rowb[:],
                start=True, stop=True), inc=self.pe)
            p.wait("vector", self.pe, tok_rb)
            p.wait("vector", self.pe, prev["lns_pe"])
            for ck in range(CK):
                sqb = self.sq[ck % 2]
                p.op("vector", lambda v, ck=ck, sqb=sqb: v.tensor_tensor(
                    sqb[:], self.xs[:, ck * SL:(ck + 1) * SL], self.ps[4][:, :],
                    op=ALU.subtract), inc=self.dve)
                p.op("vector", lambda v, ck=ck, sqb=sqb: v.tensor_tensor(
                    self.lns[:, ck * SL:(ck + 1) * SL], sqb[:], self.ps[5][:, :],
                    op=ALU.mult), inc=self.dve)
            t_ln = self.dve.n
            prev["xs_dve"] = t_ln
            p.wait("tensor", self.dve, t_ln)
            p.wait("scalar", self.dve, t_ln)

            eqs = [self.eq, self.sq[0]]
            pend = None
            lad_toks = [0, 0]
            for hq in range(HQ):
                bank = hq % 2
                eqb = eqs[hq % 2][:, 0:512]
                if hq >= 2:
                    p.wait("scalar", self.dve, lad_toks[hq % 2])
                for ck in range(CK):
                    tok_q = p.op("tensor", lambda t, ck=ck, hq=hq, bank=bank: t.matmul(
                        self.ps[bank][:, :],
                        self.wq_sb[:, ck * 1024 + hq * 128: ck * 1024 + (hq + 1) * 128],
                        self.lns[:, ck * SL:(ck + 1) * SL],
                        start=(ck == 0), stop=(ck == CK - 1)),
                        inc=self.pe if ck == CK - 1 else None)
                p.wait("scalar", self.pe, tok_q)
                p.wait_all("scalar", self.qst)
                tok_eq = self._sm_exp(bank, self.q_fm[hq * 128:(hq + 1) * 128, cols],
                                      eqb)
                if pend is not None:
                    lad_toks[(hq - 1) % 2] = self._sm_ladder(*pend)
                pend = (tok_eq, eqb, self.qh_fm[hq * 128:(hq + 1) * 128, cols])
                p.wait("tensor", self.act, tok_eq)
            lad_toks[(HQ - 1) % 2] = self._sm_ladder(*pend)

            pend = None
            lad_k = [0, 0]
            for j in range(KV + 1):
                if j < KV:
                    bank = j % 2
                    eqb = eqs[j % 2][:, 0:512]
                    p.wait("scalar", self.dve,
                           lad_k[j % 2] if j >= 2 else lad_toks[j % 2])
                    for ck in range(CK):
                        tok_k = p.op("tensor", lambda t, ck=ck, j=j, bank=bank: t.matmul(
                            self.ps[bank][:, :],
                            self.wk_sb[:, ck * 512 + j * 128: ck * 512 + (j + 1) * 128],
                            self.lns[:, ck * SL:(ck + 1) * SL],
                            start=(ck == 0), stop=(ck == CK - 1)),
                            inc=self.pe if ck == CK - 1 else None)
                    p.wait("scalar", self.pe, tok_k)
                    p.wait_all("scalar", self.qst)
                    tok_eq2 = self._sm_exp(bank, self.k_fm[j * 128:(j + 1) * 128, cols],
                                           eqb)
                    p.wait("tensor", self.act, tok_eq2)
                if pend is not None:
                    jp, peq, peqb = pend
                    lad_k[jp % 2] = self._sm_ladder(
                        peq, peqb, self.kh_fm[jp * 128:(jp + 1) * 128, cols])
                    p.wait("tensor", self.dve, lad_k[jp % 2])
                    tct = self.sq[1][:, 0:256].bitcast(BF16)
                    p.wait_all("vector", self.qst)
                    for su in range(4):
                        pt = self.psT[su % 2]
                        tok_tr = p.op("tensor", lambda t, su=su, pt=pt: t.transpose(
                            pt[:, 0:128], self.tb[:, su * 128:(su + 1) * 128],
                            self.identb[:]), inc=self.pe)
                        p.wait("vector", self.pe, tok_tr)
                        tok_ev = p.op("vector", lambda v, su=su, pt=pt, tct=tct: v.tensor_copy(
                            tct[:, su * 128:(su + 1) * 128], pt[:, 0:128]),
                            inc=self.dve)
                        if su < 3:
                            p.wait("tensor", self.dve, tok_ev)
                    t_tr = self.dve.n
                    p.wait("scalar", self.dve, t_tr)
                    p.dma("scalar",
                          self.kh_tm[jp, ts * SL:(ts + 1) * SL, :]
                          .rearrange("(su p) d -> p su d", p=128),
                          tct.rearrange("p (su d) -> p su d", su=4),
                          self.qst)
                if j < KV:
                    pend = (j, tok_eq2, eqb)

            for su in range(4):
                bank = su % 2
                for ck in range(CK):
                    tok_v = p.op("tensor", lambda t, ck=ck, su=su, bank=bank: t.matmul(
                        self.ps[bank][:, :],
                        self.lns[:, ck * SL + su * 128: ck * SL + (su + 1) * 128],
                        self.wv_sb[:, ck * 512:(ck + 1) * 512],
                        start=(ck == 0), stop=(ck == CK - 1)),
                        inc=self.pe if ck == CK - 1 else None)
                p.wait("scalar", self.pe, tok_v)
                tok_ev = p.op("scalar", lambda s, bank=bank: s.activation(
                    self.ta[:], self.ps[bank][:, :], AF.Copy), inc=self.act)
                p.wait("scalar", self.act, tok_ev)
                st_v = p.dma("scalar",
                             self.v_tm[ts * SL + su * 128: ts * SL + (su + 1) * 128, :],
                             self.ta[:], self.qst)
                p.wait("scalar", self.qst, st_v)
                p.wait("tensor", self.act, tok_ev)

            for j in range(KV):
                for ck in range(CK):
                    tok_g = p.op("tensor", lambda t, ck=ck, j=j: t.matmul(
                        self.ps[2][0:1, :], self.wg2b[:, ck * 4 + j: ck * 4 + j + 1],
                        self.lns[:, ck * SL:(ck + 1) * SL],
                        start=(ck == 0), stop=(ck == CK - 1)),
                        inc=self.pe if ck == CK - 1 else None)
                p.wait("vector", self.pe, tok_g)
                p.op("vector", lambda v: v.tensor_scalar(
                    self.rowa[:], self.ps[2][0:1, :], self.bgv[0:1, 0:1], None,
                    op0=ALU.add), inc=self.dve)
                tok_z = self.dve.n
                p.wait("scalar", self.dve, tok_z)
                p.op("scalar", lambda s: s.activation(self.rowb[:], self.rowa[:],
                                                      AF.Abs), inc=self.act)
                p.op("scalar", lambda s: s.activation(self.rowb[:], self.rowb[:],
                                                      AF.Exp, scale=self.constf(-1.0)), inc=self.act)
                p.op("scalar", lambda s: s.activation(
                    self.rowb[:], self.rowb[:], AF.Ln, bias=1.0), inc=self.act)
                tok_rl = p.op("scalar", lambda s: s.activation(
                    self.rowc[:], self.rowa[:], AF.Relu), inc=self.act)
                p.wait("vector", self.act, tok_rl)
                p.op("vector", lambda v: v.tensor_tensor(
                    self.rowc[:], self.rowa[:], self.rowc[:], op=ALU.subtract),
                    inc=self.dve)
                tok_sub = p.op("vector", lambda v: v.tensor_tensor(
                    self.rowc[:], self.rowc[:], self.rowb[:], op=ALU.subtract),
                    inc=self.dve)
                p.wait("scalar", self.dve, tok_sub)
                p.wait_all("scalar", self.qst)
                tok_gs = p.op("scalar", lambda s: s.activation(
                    self.gsst[:], self.rowc[:], AF.Copy,
                    scale=self.constf(1.0 / 16.0)), inc=self.act)
                p.wait("scalar", self.act, tok_gs)
                p.dma("scalar", self.gs_d[j:j + 1, ts * SL:(ts + 1) * SL],
                      self.gsst[:], self.qst)
                p.wait("tensor", self.act, tok_gs)
                p.wait("vector", self.act, tok_gs)
            prev["lns_pe"] = self.pe.n
            prev["xs_act"] = self.act.n
            prev["sq_dve"] = self.dve.n



    def p2(self):
        p, nc, ctx = self.p, self.nc, self.ctx
        s_ = lambda n, sh, d: ctx.enter_context(nc.sbuf_tensor(n, sh, d))
        cur = [0]

        def carve(ncols):
            a = self.big1[:, cur[0]:cur[0] + ncols]
            cur[0] += ncols
            return a
        khf4 = [carve(512) for i in range(2)]
        khtm4 = [carve(512) for i in range(2)]
        vtm4 = [carve(512) for i in range(2)]
        qpf8 = [carve(1024) for i in range(2)]
        T128 = lambda n: [carve(128) for i in range(2)]
        Pb, mb = T128("p2Pb"), T128("p2mb")
        Kp, Km, Ktil = T128("p2Kp"), T128("p2Km"), T128("p2Ktil")
        Qp0, Qp1 = T128("p2Qp0"), T128("p2Qp1")
        Atl, Atu, MTt = T128("p2Atl"), T128("p2Atu"), T128("p2MTt")
        Xa, Xb = T128("p2Xa"), T128("p2Xb")
        XTa, XTb = T128("p2XTa"), T128("p2XTb")
        t1s = T128("p2t1s")
        Utm, Wtm, Mp = T128("p2Utm"), T128("p2Wtm"), T128("p2Mp")
        ftile, ftile2 = T128("p2ft"), T128("p2ft2")
        gcol = [s_(f"p2gcol{i}", [128, 1], F32) for i in range(2)]
        gsb = [[s_(f"p2gs{j}_{i}", [1, 128], F32) for i in range(2)] for j in range(KV)]
        crows = [s_(f"p2crows{i}", [1, 256], F32) for i in range(2)]
        trow = [s_(f"p2trow{i}", [1, 128], F32) for i in range(2)]
        dcol = [s_(f"p2dcol{i}", [128, 1], F32) for i in range(2)]
        gcc = [s_(f"p2gcc{i}", [128, 1], F32) for i in range(2)]
        Sf = [s_(f"p2Sf{j}", [128, 128], F32) for j in range(KV)]
        Sb = [s_(f"p2Sb{j}", [128, 128], BF16) for j in range(KV)]
        Stmp = [s_(f"p2Stmp{i}", [128, 128], F32) for i in range(2)]
        mls = self.sq[0][:, 0:128]
        mus = self.sq[0][:, 128:256]
        mui = self.sq[0][:, 256:384]

        qld, qst = self.qld, self.qst
        qs2 = [p.sem("qs2a"), p.sem("qs2b")]
        pe, dve, act = self.pe, self.dve, self.act
        kh_r = self.kh_fm.rearrange("(j p) t -> p j t", p=128)
        ktm_r = self.kh_tm.rearrange("j t d -> t j d")
        qh_r = self.qh_fm.rearrange("(h p) t -> p h t", p=128)
        FSC = float(SCALE * FLA_W)
        self.constf(FSC)
        self.constf(2.0)

        p.dma("sync", mls[:], self.mask_ls, qld)
        p.dma("sync", mus[:], self.mask_us, qld)
        tok_m = p.dma("sync", mui[:], self.mask_ui, qld)
        p.wait("vector", qld, tok_m)

        for j in range(KV):
            p.op("vector", lambda v, j=j: v.memset(Sf[j][:], 0.0), inc=dve)
        t_s0 = dve.n
        p.wait("tensor", dve, t_s0)

        res = {}

        def use(engine, name):
            if name in res:
                sem, val = res[name]
                if val is None:
                    p.wait_all(engine, sem)
                else:
                    p.wait(engine, sem, val)

        def free(name, sem, val):
            res[name] = (sem, val)

        def load_chunk(c):
            bp = c % 2
            use("sync", f"ldbuf{bp}")
            c0 = c * CH
            q_ = qs2[bp]
            p.dma("sync", khf4[bp].rearrange("p (j t) -> p j t", j=KV),
                  kh_r[:, :, c0:c0 + CH], q_)
            p.dma("sync", khtm4[bp].rearrange("p (j d) -> p j d", j=KV),
                  ktm_r[c0:c0 + CH, :, :], q_)
            p.dma("sync", vtm4[bp], self.v_tm[c0:c0 + CH, :], q_)
            p.dma("sync", qpf8[bp].rearrange("p (h t) -> p h t", h=HQ),
                  qh_r[:, :, c0:c0 + CH], q_)
            for j in range(KV):
                p.dma("sync", gsb[j][bp][:], self.gs_d[j:j + 1, c0:c0 + CH], q_)
            return q_.n

        def head_gen(c, j, t_ld):
            bp = c % 2
            pp = j % 2
            c0 = c * CH
            psG = self.ps[0 + pp]
            psN = self.ps[2 + pp]
            psU = self.ps[4 + pp]
            psT = self.psT[pp]
            khf = khf4[bp][:, j * 128:(j + 1) * 128]
            khtm = khtm4[bp][:, j * 128:(j + 1) * 128]
            vtm = vtm4[bp][:, j * 128:(j + 1) * 128]
            qpf = [qpf8[bp][:, (2 * j + i) * 128:(2 * j + i + 1) * 128] for i in range(2)]

            use("tensor", f"psG{pp}")
            p.wait("tensor", qs2[bp], t_ld)
            tk = p.op("tensor", lambda t: t.matmul(
                psG[:, 0:1], gsb[j][bp][:], self.onescol_f[0:1, 0:1],
                start=True, stop=True), inc=pe)
            p.wait("vector", pe, tk)
            use("vector", f"gcol{pp}")
            tk = p.op("vector", lambda v: v.tensor_copy(gcol[pp][:], psG[:, 0:1]), inc=dve)
            yield
            p.wait("tensor", dve, tk)
            p.op("tensor", lambda t: t.matmul(psG[0:1, 128:256], gcol[pp][:], mui[:],
                                              start=True, stop=True), inc=pe)
            tk = p.op("tensor", lambda t: t.matmul(psG[0:1, 256:384], gcol[pp][:], mls[:],
                                                   start=True, stop=True), inc=pe)
            p.wait("vector", pe, tk)
            use("vector", f"crows{pp}")
            tk_cr = p.op("vector", lambda v: v.tensor_copy(crows[pp][:], psG[0:1, 128:384]),
                         inc=dve)
            tk_tot = p.op("vector", lambda v: v.tensor_tensor(
                trow[pp][:], crows[pp][0:1, 0:128], crows[pp][0:1, 128:256], op=ALU.add),
                inc=dve)
            yield
            p.wait("tensor", dve, tk_tot)
            p.op("tensor", lambda t: t.matmul(psG[:, 0:1], crows[pp][0:1, 128:256],
                                              self.onescol_f[0:1, 0:1], start=True, stop=True),
                 inc=pe)
            p.op("tensor", lambda t: t.matmul(psG[:, 1:2], trow[pp][:],
                                              self.onescol_f[0:1, 0:1], start=True, stop=True),
                 inc=pe)
            tk = p.op("tensor", lambda t: t.matmul(psG[:, 256:384], self.onesrow_f[0:1, 0:128],
                                                   crows[pp][0:1, 0:128], start=True, stop=True),
                      inc=pe)
            p.wait("scalar", pe, tk)
            use("scalar", f"dcol{pp}")
            p.op("scalar", lambda s: s.activation(dcol[pp][:], psG[:, 0:1], AF.Exp), inc=act)
            p.op("scalar", lambda s: s.activation(gcc[pp][:], psG[:, 1:2], AF.Exp), inc=act)
            use("scalar", f"Pb{pp}")
            p.op("scalar", lambda s: s.activation(Pb[pp], psG[:, 256:384], AF.Exp), inc=act)
            tk_g = p.op("scalar", lambda s: s.activation(
                mb[pp], psG[:, 256:384], AF.Exp, scale=self.constf(-1.0)), inc=act)
            free(f"psG{pp}", act, tk_g)
            free(f"gcol{pp}", pe, pe.n)
            free(f"crows{pp}", pe, pe.n)
            yield
            p.wait("vector", act, tk_g)
            p.wait("vector", qs2[bp], t_ld)
            use("vector", f"Kp{pp}")
            p.op("vector", lambda v: v.tensor_tensor(Kp[pp], khf, Pb[pp], op=ALU.mult),
                 inc=dve)
            p.op("vector", lambda v: v.tensor_tensor(Km[pp], khf, mb[pp], op=ALU.mult),
                 inc=dve)
            p.op("vector", lambda v: v.tensor_scalar(Ktil[pp], khtm, dcol[pp][:], None,
                                                     op0=ALU.mult), inc=dve)
            p.op("vector", lambda v: v.tensor_tensor(Qp0[pp], qpf[0], Pb[pp], op=ALU.mult),
                 inc=dve)
            tk_kp = p.op("vector", lambda v: v.tensor_tensor(Qp1[pp], qpf[1], Pb[pp],
                                                             op=ALU.mult), inc=dve)
            free(f"Pb{pp}", dve, tk_kp)
            free(f"dcol{pp}", dve, tk_kp)
            yield
            p.wait("tensor", dve, tk_kp)
            use("tensor", f"psN{pp}")
            tk = p.op("tensor", lambda t: t.matmul(psN[:, 0:128], Kp[pp], Km[pp],
                                                   start=True, stop=True), inc=pe)
            p.wait("vector", pe, tk)
            use("vector", f"Atl{pp}")
            tk_atl = p.op("vector", lambda v: v.tensor_tensor(Atl[pp], psN[:, 0:128], mls[:],
                                                              op=ALU.mult), inc=dve)
            p.wait("tensor", dve, tk_atl)
            tk = p.op("tensor", lambda t: t.matmul(psN[:, 0:128], Km[pp], Kp[pp],
                                                   start=True, stop=True), inc=pe)
            p.wait("vector", pe, tk)
            tk_atu = p.op("vector", lambda v: v.tensor_tensor(Atu[pp], psN[:, 0:128], mus[:],
                                                              op=ALU.mult), inc=dve)
            p.op("vector", lambda v: v.tensor_tensor(Xa[pp], self.identb[:], Atl[pp],
                                                     op=ALU.subtract), inc=dve)
            p.op("vector", lambda v: v.tensor_tensor(XTa[pp], self.identb[:], Atu[pp],
                                                     op=ALU.subtract), inc=dve)
            tk_mt = p.op("vector", lambda v: v.tensor_tensor(MTt[pp], self.identb[:],
                                                             Atu[pp], op=ALU.add), inc=dve)
            free(f"Atl{pp}", dve, tk_mt)
            yield
            X, XT = Xa[pp], XTa[pp]
            Xn, XTn = Xb[pp], XTb[pp]
            p.wait("tensor", dve, tk_mt)
            for it in range(6):
                tk = p.op("tensor", lambda t, X=X: t.matmul(psN[:, 0:128], MTt[pp], X,
                                                            start=True, stop=True), inc=pe)
                p.wait("scalar", pe, tk)
                tk_t1 = p.op("scalar", lambda s: s.activation(t1s[pp], psN[:, 0:128],
                                                              AF.Copy), inc=act)
                p.wait("tensor", act, tk_t1)
                tk = p.op("tensor", lambda t, XT=XT: t.matmul(psN[:, 128:256], XT, t1s[pp],
                                                              start=True, stop=True), inc=pe)
                p.wait("vector", pe, tk)
                tk_x = p.op("vector", lambda v, X=X, Xn=Xn: v.scalar_tensor_tensor(
                    Xn, X, 2.0, psN[:, 128:256], op0=ALU.mult, op1=ALU.subtract), inc=dve)
                p.wait("tensor", dve, tk_x)
                tk = p.op("tensor", lambda t, XT=XT: t.matmul(psN[:, 256:384], t1s[pp], XT,
                                                              start=True, stop=True), inc=pe)
                p.wait("vector", pe, tk)
                tk_xt = p.op("vector", lambda v, XT=XT, XTn=XTn: v.scalar_tensor_tensor(
                    XTn, XT, 2.0, psN[:, 256:384], op0=ALU.mult, op1=ALU.subtract), inc=dve)
                p.wait("tensor", dve, tk_xt)
                X, Xn = Xn, X
                XT, XTn = XTn, XT
                yield
            free(f"psN{pp}", dve, dve.n)
            use("tensor", f"psU{pp}")
            if c == 0:
                Uop = vtm
            else:
                use("tensor", f"Sb{j}")
                tk = p.op("tensor", lambda t: t.matmul(psU[:, 0:128], Kp[pp], Sb[j][:],
                                                       start=True, stop=True), inc=pe)
                p.wait("vector", pe, tk)
                use("vector", f"Utm{pp}")
                tk_u = p.op("vector", lambda v: v.tensor_tensor(Utm[pp], vtm, psU[:, 0:128],
                                                                op=ALU.subtract), inc=dve)
                p.wait("tensor", dve, tk_u)
                Uop = Utm[pp]
                yield
            tk = p.op("tensor", lambda t, XT=XT, Uop=Uop: t.matmul(
                psU[:, 0:128], XT, Uop, start=True, stop=True), inc=pe)
            p.wait("scalar", pe, tk)
            use("scalar", f"Wtm{pp}")
            tk_w = p.op("scalar", lambda s: s.activation(Wtm[pp], psU[:, 0:128], AF.Copy),
                        inc=act)
            p.wait("tensor", act, tk_w)
            yield
            for i in range(2):
                Qpi = (Qp0, Qp1)[i][pp]
                use("tensor", f"psN{pp}")
                tk = p.op("tensor", lambda t, Qpi=Qpi: t.matmul(psN[:, 0:128], Km[pp], Qpi,
                                                                start=True, stop=True), inc=pe)
                p.wait("vector", pe, tk)
                use("vector", f"Mp{pp}")
                tk_mp = p.op("vector", lambda v: v.tensor_tensor(Mp[pp], psN[:, 0:128],
                                                                 mui[:], op=ALU.mult), inc=dve)
                free(f"psN{pp}", dve, tk_mp)
                p.wait("tensor", dve, tk_mp)
                if c == 0:
                    tk = p.op("tensor", lambda t: t.matmul(psU[:, 128:256], Mp[pp],
                                                           Wtm[pp], start=True, stop=True),
                              inc=pe)
                else:
                    p.op("tensor", lambda t, Qpi=Qpi: t.matmul(psU[:, 128:256], Qpi, Sb[j][:],
                                                               start=True, stop=False), inc=pe)
                    tk = p.op("tensor", lambda t: t.matmul(psU[:, 128:256], Mp[pp],
                                                           Wtm[pp], start=False, stop=True),
                              inc=pe)
                p.wait("scalar", pe, tk)
                use("scalar", f"ftile{pp}")
                tk_f = p.op("scalar", lambda s: s.activation(ftile[pp], psU[:, 128:256],
                                                             AF.Copy, scale=FSC), inc=act)
                p.wait("tensor", act, tk_f)
                use("tensor", f"psT{pp}")
                tk = p.op("tensor", lambda t: t.transpose(psT[:, 0:128], ftile[pp],
                                                          self.identb[:]), inc=pe)
                free(f"ftile{pp}", pe, pe.n)
                p.wait("vector", pe, tk)
                use("vector", f"ftile2{pp}")
                tk_f2 = p.op("vector", lambda v: v.tensor_copy(ftile2[pp], psT[:, 0:128]),
                             inc=dve)
                free(f"psT{pp}", dve, tk_f2)
                p.wait("scalar", dve, tk_f2)
                shard = c0 // TC
                cc = c0 % TC
                hq = 2 * j + i
                p.dma("scalar",
                      self.contrib[shard * 1024 + hq * 128: shard * 1024 + (hq + 1) * 128,
                                   cc // 2:(cc + CH) // 2],
                      ftile2[pp].bitcast(F32), qst)
                free(f"ftile2{pp}", qst, None)
                yield
            tk = p.op("tensor", lambda t: t.matmul(psU[:, 0:128], Ktil[pp], Wtm[pp],
                                                   start=True, stop=True), inc=pe)
            free(f"Wtm{pp}", pe, pe.n)
            free(f"Kp{pp}", pe, pe.n)
            free(f"Mp{pp}", pe, pe.n)
            free(f"Utm{pp}", pe, pe.n)
            p.wait("vector", pe, tk)
            if c == 0:
                tk_s = p.op("vector", lambda v: v.tensor_copy(Sf[j][:], psU[:, 0:128]), inc=dve)
            else:
                p.op("vector", lambda v: v.tensor_scalar(Stmp[pp][:], Sf[j][:], gcc[pp][:],
                                                         None, op0=ALU.mult), inc=dve)
                tk_s = p.op("vector", lambda v: v.tensor_tensor(Sf[j][:], Stmp[pp][:],
                                                                psU[:, 0:128], op=ALU.add),
                            inc=dve)
            free(f"psU{pp}", dve, tk_s)
            p.wait("scalar", dve, tk_s)
            tk_sb = p.op("scalar", lambda s: s.activation(Sb[j][:], Sf[j][:], AF.Copy), inc=act)
            res[f"Sb{j}"] = (act, tk_sb)
            yield

        p.wait("sync", qst, qst.n)
        t_ld = load_chunk(0)
        for c in range(NCH):
            if c + 1 < NCH:
                t_ld_next = load_chunk(c + 1)
            for jp in range(2):
                alive = [head_gen(c, j, t_ld) for j in (2 * jp, 2 * jp + 1)]
                while alive:
                    nxt = []
                    for g in alive:
                        try:
                            next(g)
                            nxt.append(g)
                        except StopIteration:
                            pass
                    alive = nxt
            free(f"ldbuf{c % 2}", self.pe, self.pe.n)
            if c + 1 < NCH:
                t_ld = t_ld_next


    def p3(self):
        p, nc, ctx = self.p, self.nc, self.ctx
        qst = self.qst
        pe, dve, act = self.pe, self.dve, self.act
        q3 = [p.sem("q3a"), p.sem("q3b")]
        qq3 = [p.sem("qq3a"), p.sem("qq3b")]
        Kres = self.big1[:, 0:4096]
        vbuf = self.big1[:, 4096:8224]
        vb3 = vbuf.rearrange("p (n e) -> p n e", e=129)
        Qt = [self.big1[:, 8224 + i * 512: 8224 + (i + 1) * 512] for i in range(2)]
        Pt = [self.big1[:, 9248 + i * 512: 9248 + (i + 1) * 512] for i in range(2)]
        ot = self.big1[:, 10272:10400]
        ot2 = self.big1[:, 10400:10528]
        cm_sb = self.xs[:, 2048:4096]
        rcol = self.ctx.enter_context(nc.sbuf_tensor("p3rcol", [128, 1], F32))

        tok_cm = p.dma("sync", cm_sb.rearrange("p (s q) -> p s q", s=4),
                       self.cmask.rearrange("s p q -> p s q"), self.qld)
        p.wait("vector", self.qld, tok_cm)
        tok_on = p.op("vector", lambda v: v.memset(vb3[:, :, 128:129], 1.0))
        p.wait("sync", dve, tok_on)
        res = {}

        def use(engine, name):
            if name in res:
                sem, val = res[name]
                if val is None:
                    p.wait_all(engine, sem)
                else:
                    p.wait(engine, sem, val)

        def free(name, sem, val):
            res[name] = (sem, val)

        for j in range(KV):
            jq = q3[j % 2]
            use("sync", "kv")
            p.dma("sync", Kres, self.k_fm[j * 128:(j + 1) * 128, :], jq)
            p.dma("sync", vb3[:, :, 0:128],
                  self.v_tm[:, j * 128:(j + 1) * 128].rearrange("(n p) d -> p n d", p=128),
                  jq)
            t_kv = jq.n
            p.wait("tensor", jq, t_kv)
            for i in range(2):
                hq = 2 * j + i
                for qg in range(8):
                    qp = (hq * 8 + qg) % 2
                    use("sync", f"qt{qp}")
                    qqs = qq3[qp]
                    p.dma("sync", Qt[qp], self.q_fm[hq * 128:(hq + 1) * 128,
                                                    qg * 512:(qg + 1) * 512], qqs)
                    t_q = qqs.n
                    p.wait("tensor", qqs, t_q)
                    nkt = 4 * qg + 4

                    def score_mm(kt):
                        sp = kt % 2
                        psc = self.ps[sp]
                        use("tensor", f"psc{sp}")
                        return p.op("tensor", lambda t, kt=kt, qp=qp, psc=psc: t.matmul(
                            psc[:, :], Kres[:, kt * 128:(kt + 1) * 128], Qt[qp],
                            start=True, stop=True), inc=pe)
                    sc_tok = {0: score_mm(0)}
                    for kt in range(nkt):
                        sp = kt % 2
                        psc = self.ps[sp]
                        if kt + 1 < nkt:
                            sc_tok[kt + 1] = score_mm(kt + 1)
                        p.wait("scalar", pe, sc_tok[kt])
                        s_ = kt - 4 * qg
                        use("scalar", f"Pt{sp}")
                        if s_ < 0:
                            tk_e = p.op("scalar", lambda s, sp=sp, psc=psc: s.activation(
                                Pt[sp], psc[:, :], AF.Exp, scale=self.constf(SCALE)), inc=act)
                            free(f"psc{sp}", act, tk_e)
                            p.wait("tensor", act, tk_e)
                            t_pt = tk_e
                        else:
                            tk_e = p.op("scalar", lambda s, sp=sp, psc=psc: s.activation(
                                Pt[sp], psc[:, :], AF.Exp, scale=self.constf(SCALE)), inc=act)
                            free(f"psc{sp}", act, tk_e)
                            p.wait("vector", act, tk_e)
                            tk_m = p.op("vector", lambda v, sp=sp, s_=s_: v.tensor_tensor(
                                Pt[sp], Pt[sp], cm_sb[:, s_ * 512:(s_ + 1) * 512],
                                op=ALU.mult), inc=dve)
                            p.wait("tensor", dve, tk_m)
                            t_pt = tk_m
                        for su in range(4):
                            if kt > 4 * qg + su:
                                continue
                            po = self.ps[2 + su]
                            if kt == 0:
                                use("tensor", f"po{su}")
                            tk = p.op("tensor", lambda t, su=su, sp=sp, po=po, kt=kt,
                                      qg=qg: t.matmul(
                                po[:, 0:129], Pt[sp][:, su * 128:(su + 1) * 128],
                                vb3[:, kt, :], start=(kt == 0),
                                stop=(kt == 4 * qg + su)), inc=pe)
                        free(f"Pt{sp}", pe, pe.n)
                    free(f"qt{qp}", pe, pe.n)
                    for su in range(4):
                        po = self.ps[2 + su]
                        p.wait("vector", pe, pe.n)
                        tk_r = p.op("vector", lambda v, po=po: v.reciprocal(
                            rcol[:], po[:, 128:129]), inc=dve)
                        use("vector", "ot")
                        tk_o = p.op("vector", lambda v, po=po: v.tensor_scalar(
                            ot, po[:, 0:128], rcol[:], None, op0=ALU.mult),
                            inc=dve)
                        free(f"po{su}", dve, tk_o)
                        p.wait("tensor", dve, tk_o)
                        use("tensor", "psT0")
                        tk = p.op("tensor", lambda t: t.transpose(
                            self.psT[0][:, 0:128], ot, self.identb[:]), inc=pe)
                        free("ot", pe, pe.n)
                        p.wait("vector", pe, tk)
                        use("vector", "ot2")
                        tk_o2 = p.op("vector", lambda v: v.tensor_copy(
                            ot2, self.psT[0][:, 0:128]), inc=dve)
                        free("psT0", dve, tk_o2)
                        p.wait("scalar", dve, tk_o2)
                        qcol = qg * 512 + su * 128
                        shard = qcol // TC
                        cc = qcol % TC
                        p.dma("scalar",
                              self.contribW[shard * 1024 + hq * 128:
                                            shard * 1024 + (hq + 1) * 128,
                                            cc // 2: cc // 2 + 64],
                              ot2.bitcast(F32), qst)
                        free("ot2", qst, None)
            free("kv", self.pe, self.pe.n)


    def p4f(self):
        p = self.p
        self.ccF = p.sem("ccF")
        p.wait_all("gpsimd", self.qst)
        self.tok_ccF = p.op("gpsimd", lambda g: g.collective_compute(
            "AllGather", ALU.bypass,
            replica_groups=[[0, 1], [2, 3], [4, 5], [6, 7]],
            ins=[self.contrib], outs=[self.gath]), inc=self.ccF)

    def p4(self):
        p = self.p
        cc = p.sem("ccol")
        p.wait_all("gpsimd", self.qst)
        tok = p.op("gpsimd", lambda g: g.collective_compute(
            "AllGather", ALU.bypass,
            replica_groups=[[0, 1], [2, 3], [4, 5], [6, 7]],
            ins=[self.contribW], outs=[self.gathW]), inc=cc)
        for e in ("sync", "tensor", "gpsimd"):
            p.wait(e, cc, tok)
            p.wait(e, self.ccF, self.tok_ccF)

    def ln_stream(self, src, dst_sbuf, dbg_out=None):
        p, nc = self.p, self.nc
        pe, dve, act = self.pe, self.dve, self.act
        qln = [p.sem("qln0"), p.sem("qln1")]
        src_r = src.rearrange("(ck p) t -> p ck t", p=128)
        xld = [self.xs[:, i * 512:(i + 1) * 512] for i in range(2)]
        ld_free = [0, 0]
        sq_free = [0]
        for tt in range(4):
            cols = slice(tt * 512, (tt + 1) * 512)
            for pas in ("stats", "norm"):
                for ck in range(CK):
                    bp = ck % 2
                    p.wait("sync", pe, ld_free[bp])
                    tok_x = p.dma("sync", xld[bp], src_r[:, ck, cols], qln[bp])
                    p.wait("tensor", qln[bp], tok_x)
                    if pas == "stats":
                        p.op("tensor", lambda t, bp=bp, ck=ck: t.matmul(
                            self.ps[2][0:1, :], self.onescol_f[:], xld[bp],
                            start=(ck == 0), stop=(ck == CK - 1)),
                            inc=pe if ck == CK - 1 else None)
                        p.wait("scalar", qln[bp], tok_x)
                        p.wait("scalar", pe, sq_free[0])
                        tok_sq = p.op("scalar", lambda s, bp=bp: s.activation(
                            self.sq[1][:], xld[bp], AF.Square), inc=act)
                        p.wait("tensor", act, tok_sq)
                        tok_mm = p.op("tensor", lambda t, ck=ck: t.matmul(
                            self.ps[3][0:1, :], self.onescol_f[:], self.sq[1][:],
                            start=(ck == 0), stop=(ck == CK - 1)), inc=pe)
                        ld_free[bp] = tok_mm
                        sq_free[0] = tok_mm
                    else:
                        p.wait("vector", qln[bp], tok_x)
                        p.op("vector", lambda v, bp=bp: v.tensor_tensor(
                            self.eq[:], xld[bp], self.ps[4][:, :], op=ALU.subtract),
                            inc=dve)
                        tok_n = p.op("vector", lambda v, ck=ck, tt=tt: v.tensor_tensor(
                            dst_sbuf[:, ck * TC + tt * 512: ck * TC + (tt + 1) * 512],
                            self.eq[:], self.ps[5][:, :], op=ALU.mult), inc=dve)
                        self.streams_pl = None
                        ld_free[bp] = 0
                        p.wait("sync", dve, tok_n)
                        if dbg_out is not None:
                            p.wait("scalar", dve, tok_n)
                            p.dma("scalar",
                                  dbg_out.rearrange("(c2 p) t -> p c2 t", p=128)[:, ck, cols],
                                  dst_sbuf[:, ck * TC + tt * 512: ck * TC + (tt + 1) * 512],
                                  self.qst)
                if pas == "stats":
                    t_stats = pe.n
                    p.wait("vector", pe, t_stats)
                    p.op("vector", lambda v: v.tensor_scalar(
                        self.rowa[:], self.ps[2][0:1, :], 1.0 / C, None, op0=ALU.mult),
                        inc=dve)
                    p.op("vector", lambda v: v.tensor_scalar(
                        self.rowb[:], self.ps[3][0:1, :], 1.0 / C, None, op0=ALU.mult),
                        inc=dve)
                    p.op("vector", lambda v: v.tensor_tensor(
                        self.rowc[:], self.rowa[:], self.rowa[:], op=ALU.mult), inc=dve)
                    tok_var = p.op("vector", lambda v: v.tensor_tensor(
                        self.rowb[:], self.rowb[:], self.rowc[:], op=ALU.subtract),
                        inc=dve)
                    p.wait("scalar", dve, tok_var)
                    p.op("scalar", lambda s: s.activation(
                        self.rowb[:], self.rowb[:], AF.Ln, bias=self.constf(1e-5)),
                        inc=act)
                    tok_rstd = p.op("scalar", lambda s: s.activation(
                        self.rowb[:], self.rowb[:], AF.Exp, scale=self.constf(-0.5)),
                        inc=act)
                    p.wait("tensor", dve, tok_var)
                    p.wait("tensor", act, tok_rstd)
                    p.op("tensor", lambda t: t.matmul(
                        self.ps[4][:, :], self.onesrow_f[0:1, 0:128], self.rowa[:],
                        start=True, stop=True), inc=pe)
                    tok_rb = p.op("tensor", lambda t: t.matmul(
                        self.ps[5][:, :], self.onesrow_f[0:1, 0:128], self.rowb[:],
                        start=True, stop=True), inc=pe)
                    p.wait("vector", pe, tok_rb)
            p.wait("tensor", self.dve, self.dve.n)

    def p5(self):
        p, nc = self.p, self.nc
        pe, dve, act, qld, qst, qgp = (self.pe, self.dve, self.act, self.qld,
                                       self.qst, self.qgp)
        attn_res = self.big1[:, 0:32768]
        fla_t = [self.ta[:], self.tb[:]]
        blA = self.lns[:, 4096:6144]
        blB = self.lns[:, 6144:8192]
        blB2 = [self.lns[:, 4096 + i * 512:4096 + (i + 1) * 512] for i in range(2)]
        wo_sl = [self.lns[:, i * 2048:(i + 1) * 2048] for i in range(2)]
        x1t = self.xs[:, 1024:1536]
        xld = [self.xs[:, i * 512:(i + 1) * 512] for i in range(2)]
        x_r = self.x_own.rearrange("(ck p) t -> p ck t", p=128)
        wo_r = self.wo.rearrange("(ck p) d -> p ck d", p=128)
        psl = self.ctx.enter_context(self.nc.sbuf_tensor("psl", [128, 2], F32))
        q5 = [p.sem("q5a"), p.sem("q5b")]
        qw5 = [p.sem("qw5a"), p.sem("qw5b")]
        qpsl = p.sem("qpsl")
        tok_ps = p.dma("sync", psl[:], self.psel, qpsl)
        p.wait("vector", qpsl, tok_ps)

        ar3 = attn_res.rearrange("p (hd t) -> p hd t", hd=CK)
        t_bl = 0
        bl_tok = [0] * CK
        for hd in range(CK):
            r_, hq = hd // 8, hd % 8
            p.wait("sync", self.dve, t_bl)
            tA = p.dma("sync", blA.bitcast(F32),
                       self.gathW[(2 * r_ + 0) * 1024 + hq * 128:
                                  (2 * r_ + 0) * 1024 + (hq + 1) * 128, :], qld)
            tB = p.dma("sync", blB.bitcast(F32),
                       self.gathW[(2 * r_ + 1) * 1024 + hq * 128:
                                  (2 * r_ + 1) * 1024 + (hq + 1) * 128, :], qld)
            p.wait("vector", qld, tB)
            p.op("vector", lambda v: v.tensor_scalar(
                blA, blA, psl[:, 0:1], None, op0=ALU.mult), inc=dve)
            t_bl = p.op("vector", lambda v, hd=hd: v.scalar_tensor_tensor(
                ar3[:, hd, :], blB, psl[:, 1:2], blA, op0=ALU.mult, op1=ALU.add),
                inc=dve)
            bl_tok[hd] = t_bl
        p.wait("sync", self.dve, t_bl)
        wo_free = [0, 0]
        xld_free = [0, 0]
        bank_free = [0, 0]
        for dc in range(CK):
            wp = dc % 2
            p.wait("gpsimd", pe, wo_free[wp])
            tok_w = p.dma("gpsimd", wo_sl[wp].rearrange("p (ck d) -> p ck d", ck=CK),
                          wo_r[:, :, dc * 128:(dc + 1) * 128], qw5[wp])
            p.wait("tensor", qw5[wp], tok_w)
            for tt in range(4):
                bank = tt % 2
                p.wait("tensor", dve, bank_free[bank])
                for hd in range(CK):
                    if dc == 0 and tt == 0:
                        p.wait("tensor", dve, bl_tok[hd])
                    tok_mm = p.op("tensor", lambda t, hd=hd, tt=tt, bank=bank, wp=wp: t.matmul(
                        self.ps[bank][:, :], wo_sl[wp][:, hd * 128:(hd + 1) * 128],
                        attn_res[:, hd * TC + tt * 512: hd * TC + (tt + 1) * 512],
                        start=(hd == 0), stop=(hd == CK - 1)),
                        inc=pe if hd == CK - 1 else None)
                xp = tt % 2
                p.wait("sync", dve, xld_free[xp])
                tok_x = p.dma("sync", xld[xp], x_r[:, dc, tt * 512:(tt + 1) * 512], q5[xp])
                p.dma("sync", fla_t[xp].bitcast(F32),
                      self.gath[2 * (dc // 8) * 1024 + (dc % 8) * 128:
                                2 * (dc // 8) * 1024 + (dc % 8 + 1) * 128,
                                tt * 256:(tt + 1) * 256], q5[xp])
                p.dma("sync", blB2[xp].bitcast(F32),
                      self.gath[(2 * (dc // 8) + 1) * 1024 + (dc % 8) * 128:
                                (2 * (dc // 8) + 1) * 1024 + (dc % 8 + 1) * 128,
                                tt * 256:(tt + 1) * 256], q5[xp])
                tok_x = q5[xp].n
                p.wait("vector", pe, tok_mm)
                p.wait("vector", q5[xp], tok_x)
                p.op("vector", lambda v, xp=xp: v.tensor_scalar(
                    fla_t[xp], fla_t[xp], psl[:, 0:1], None, op0=ALU.mult), inc=dve)
                p.op("vector", lambda v, xp=xp: v.scalar_tensor_tensor(
                    fla_t[xp], blB2[xp], psl[:, 1:2], fla_t[xp],
                    op0=ALU.mult, op1=ALU.add), inc=dve)
                tok_c = p.op("vector", lambda v, bank=bank, xp=xp: v.scalar_tensor_tensor(
                    x1t[:], self.ps[bank][:, :], 0.99, fla_t[xp],
                    op0=ALU.mult, op1=ALU.add), inc=dve)
                bank_free[bank] = tok_c
                tok_x1 = p.op("vector", lambda v, xp=xp: v.tensor_tensor(
                    x1t[:], x1t[:], xld[xp], op=ALU.add), inc=dve)
                xld_free[xp] = tok_x1
                p.wait("scalar", dve, tok_x1)
                p.dma("scalar", self.x1r.rearrange("(ck p) t -> p ck t", p=128)
                      [:, dc, tt * 512:(tt + 1) * 512], x1t[:], qst)
                p.wait_all("vector", qst)
            wo_free[wp] = pe.n
        p.wait_all("sync", qst)
        self.ln_stream(self.x1r, self.big0,
                       dbg_out=self.ln_d if "ln_d" in self.dbg else None)


    def p0b(self):
        p, nc = self.p, self.nc
        qc = [p.sem("qc0"), p.sem("qc1")]
        bc = [self.big1[:, 0:16384], self.big1[:, 16384:32768]]
        jobs = []
        for e in range(8):
            s_ = self.wguI[e].rearrange("(ck p) d -> p ck d", p=128)
            d_ = self.wguC[e].rearrange("(ck p) d -> p ck d", p=128)
            jobs.append((s_[:, :, 0:1024], d_[:, :, 0:1024], (CK, 1024)))
            jobs.append((s_[:, :, 1024:2048], d_[:, :, 1024:2048], (CK, 1024)))
        for e in range(8):
            s_ = self.wdT[e].rearrange("(hk p) d -> p hk d", p=128)
            d_ = self.wdC[e].rearrange("(hk p) d -> p hk d", p=128)
            jobs.append((s_, d_, (8, 2048)))
        sg_ = self.shguI.rearrange("(ck p) d -> p ck d", p=128)
        dg_ = self.shguC.rearrange("(ck p) d -> p ck d", p=128)
        for q in range(4):
            jobs.append((sg_[:, :, q * 1024:(q + 1) * 1024],
                         dg_[:, :, q * 1024:(q + 1) * 1024], (CK, 1024)))
        sd_ = self.shdT.rearrange("(hk p) d -> p hk d", p=128)
        dd_ = self.shdC.rearrange("(hk p) d -> p hk d", p=128)
        for q in range(2):
            jobs.append((sd_[:, :, q * 1024:(q + 1) * 1024],
                         dd_[:, :, q * 1024:(q + 1) * 1024], (CK, 1024)))
        tok = [0, 0]
        for i, (s_, d_, (n1, n2)) in enumerate(jobs):
            bp = i % 2
            buf = bc[bp][:, 0:n1 * n2].rearrange("p (a b) -> p a b", a=n1)
            p.wait("gpsimd", qc[bp], tok[bp])
            tl = p.dma("gpsimd", buf, s_, qc[bp])
            p.wait("gpsimd", qc[bp], tl)
            tok[bp] = p.dma("gpsimd", buf if False else d_, buf, qc[bp]) or qc[bp].n
        self.tok_cast_done = [qc[0].n, qc[1].n]
        self.qc = qc

    def cast_wait(self):
        for e in ENGINES:
            self.p.wait(e, self.qc[0], self.tok_cast_done[0])
            self.p.wait(e, self.qc[1], self.tok_cast_done[1])

    def ffn(self, src_res, dst_res, tag):
        p, nc = self.p, self.nc
        pe, dve, act, qld, qst, qgp = (self.pe, self.dve, self.act, self.qld,
                                       self.qst, self.qgp)
        lnb = self.big0
        cbc = [self.lns[:, e * 512:(e + 1) * 512] for e in range(8)]
        wd_sl = [self.lns[:, 4096 + i * 1024: 4096 + (i + 1) * 1024] for i in range(2)]
        sgb = self.lns[:, 6144:6656]
        gta = [self.lns[:, 6656 + i * 512: 6656 + (i + 1) * 512] for i in range(2)]
        ish_sb = self.xs[:, 1536:2560]
        xstr = [self.xs[:, 2560 + i * 512:2560 + (i + 1) * 512] for i in range(2)]
        xtmp = self.xs[:, 3584:4096]
        wgu = self.xs[:, 4096:8192].bitcast(BF16)
        gu_sl = [wgu[:, i * 4096:(i + 1) * 4096] for i in range(2)]
        gw_sb = self.sq[0][:, 384:448].bitcast(BF16)
        sc_em = self.eq[0:8, :]
        sc_tm = self.sq[0][:, 0:32]
        cb_tm = self.sq[0][:, 32:64]
        m1c = self.sq[0][:, 64:65]
        m2c = self.sq[0][:, 65:66]
        w1c = self.sq[0][:, 66:67]
        w2c = self.sq[0][:, 67:68]
        msk = self.sq[0][:, 68:100]
        qg6 = [p.sem(tag + "qg0"), p.sem(tag + "qg1")]
        qd6 = [p.sem(tag + "qd0"), p.sem(tag + "qd1")]
        q6x = [p.sem(tag + "qx0"), p.sem(tag + "qx1")]
        qmisc = p.sem(tag + "qm")

        tok_i = p.dma("sync", ish_sb, self.ishift, qmisc)
        tok_g = p.dma("gpsimd", gw_sb.rearrange("p (ck e) -> p ck e", ck=CK),
                      self.gwT.rearrange("(ck p) e -> p ck e", p=128), qgp)
        seg_sb = self.sq[0][:, 100:108].bitcast(BF16)
        tok_s = p.dma("gpsimd", seg_sb, self.segR, qgp)
        p.wait("tensor", qgp, tok_s)
        p.wait("tensor", qmisc, tok_i)
        src_r = src_res.rearrange("(ck p) t -> p ck t", p=128)
        dst_r = dst_res.rearrange("(ck p) t -> p ck t", p=128)
        wguC_r = self.wguC.rearrange("e (ck p) d -> e p ck d", p=128)
        wdC_r = self.wdC.rearrange("e (hk p) d -> e p hk d", p=128)
        shgu_r = self.shguC.rearrange("(ck p) d -> p ck d", p=128)
        shd_r = self.shdC.rearrange("(hk p) d -> p hk d", p=128)

        gu_free = [0, 0]
        gubA = [0, 0, 0, 0]
        gubD = [0, 0, 0, 0]
        gta_free = [0, 0]
        wd_free = [0, 0]
        x_free = [0, 0]
        dbank_free = [0, 0]

        def gate_up(dst_slot, wsrc, gi, cbe, lnt):
            wp = gi % 2
            p.wait("sync", pe, gu_free[wp])
            tok_w = p.dma("sync", gu_sl[wp].rearrange("p (ck d) -> p ck d", ck=CK),
                          wsrc, qg6[wp])
            p.wait("tensor", qg6[wp], tok_w)
            b0 = 2 * (gi % 2)
            b1 = b0 + 1
            p.wait("tensor", act, gubA[b0])
            p.wait("tensor", dve, gubD[b0])
            p.wait("tensor", act, gubA[b1])
            p.wait("tensor", dve, gubD[b1])
            for ck in range(CK):
                p.op("tensor", lambda t, ck=ck, wp=wp, b0=b0: t.matmul(
                    self.ps[b0][:, :], gu_sl[wp][:, ck * 256: ck * 256 + 128],
                    lnt(ck), start=(ck == 0), stop=(ck == CK - 1)),
                    inc=pe if ck == CK - 1 else None)
            for ck in range(CK):
                tok_u = p.op("tensor", lambda t, ck=ck, wp=wp, b1=b1: t.matmul(
                    self.ps[b1][:, :], gu_sl[wp][:, ck * 256 + 128: ck * 256 + 256],
                    lnt(ck), start=(ck == 0), stop=(ck == CK - 1)),
                    inc=pe if ck == CK - 1 else None)
            gu_free[wp] = tok_u
            ga = gta[gi % 2]
            p.wait("scalar", pe, tok_u)
            p.wait("scalar", dve, gta_free[gi % 2])
            tok_si = p.op("scalar", lambda s, b0=b0, ga=ga: s.activation(
                ga, self.ps[b0][:, :], AF.Silu), inc=act)
            gubA[b0] = tok_si
            p.wait("vector", act, tok_si)
            if cbe is None:
                tok_h = p.op("vector", lambda v, dst_slot=dst_slot, b1=b1,
                             ga=ga: v.tensor_tensor(
                    dst_slot, ga, self.ps[b1][:, :], op=ALU.mult), inc=dve)
            else:
                p.op("vector", lambda v, b1=b1, ga=ga: v.tensor_tensor(
                    ga, ga, self.ps[b1][:, :], op=ALU.mult), inc=dve)
                tok_h = p.op("vector", lambda v, dst_slot=dst_slot, cbe=cbe,
                             ga=ga: v.tensor_tensor(
                    dst_slot, ga, cbe, op=ALU.mult), inc=dve)
            gubD[b1] = tok_h
            gta_free[gi % 2] = tok_h
            return tok_h

        def comb_tile(tt):
            tc = slice(tt * 512, (tt + 1) * 512)
            lnt = lambda ck, tt=tt: lnb[:, ck * TC + tt * 512:
                                        ck * TC + (tt + 1) * 512]
            for ck in range(CK):
                tok_sc = p.op("tensor", lambda t, ck=ck: t.matmul(
                    self.ps[4][0:8, :], gw_sb[:, ck * 8:(ck + 1) * 8], lnt(ck),
                    start=(ck == 0), stop=(ck == CK - 1)),
                    inc=pe if ck == CK - 1 else None)
            p.wait("vector", pe, tok_sc)
            tok_em = p.op("vector", lambda v: v.tensor_copy(sc_em, self.ps[4][0:8, :]),
                          inc=dve)
            p.wait("tensor", dve, tok_em)
            for su in range(4):
                tok_tr = p.op("tensor", lambda t, su=su: t.transpose(
                    self.ps[5][:, 0:8], sc_em[:, su * 128:(su + 1) * 128],
                    self.identf[0:8, 0:8]), inc=pe)
                p.wait("vector", pe, tok_tr)
                tok_ev = p.op("vector", lambda v, su=su: v.tensor_copy(
                    sc_tm[:, su * 8:(su + 1) * 8], self.ps[5][:, 0:8]), inc=dve)
                p.wait("tensor", dve, tok_ev)
            for su in range(4):
                s8 = sc_tm[:, su * 8:(su + 1) * 8]
                c8 = cb_tm[:, su * 8:(su + 1) * 8]
                mk = msk[:, su * 8:(su + 1) * 8]
                p.op("vector", lambda v, s8=s8: v.tensor_reduce(
                    m1c, s8, axis=AX, op=ALU.max), inc=dve)
                p.op("vector", lambda v, s8=s8, mk=mk: v.tensor_scalar(
                    mk, s8, m1c, None, op0=ALU.is_ge), inc=dve)
                p.op("vector", lambda v, s8=s8, mk=mk, c8=c8: v.scalar_tensor_tensor(
                    c8, mk, -1e30, s8, op0=ALU.mult, op1=ALU.add), inc=dve)
                p.op("vector", lambda v, c8=c8: v.tensor_reduce(
                    m2c, c8, axis=AX, op=ALU.max), inc=dve)
                tok_d = p.op("vector", lambda v: v.tensor_tensor(
                    w1c, m1c, m2c, op=ALU.subtract), inc=dve)
                p.wait("scalar", dve, tok_d)
                p.op("scalar", lambda s: s.activation(w1c, w1c, AF.Sigmoid), inc=act)
                tok_w2 = p.op("scalar", lambda s: s.activation(
                    w2c, w1c, AF.Copy, scale=self.constf(-1.0), bias=1.0), inc=act)
                p.wait("vector", act, tok_w2)
                p.op("vector", lambda v, c8=c8: v.tensor_scalar(
                    c8, c8, m2c, None, op0=ALU.is_ge), inc=dve)
                p.op("vector", lambda v, c8=c8: v.tensor_scalar(
                    c8, c8, w2c, None, op0=ALU.mult), inc=dve)
                p.op("vector", lambda v, mk=mk: v.tensor_scalar(
                    mk, mk, w1c, None, op0=ALU.mult), inc=dve)
                tok_cb = p.op("vector", lambda v, c8=c8, mk=mk: v.tensor_tensor(
                    c8, c8, mk, op=ALU.add), inc=dve)
            for e in range(8):
                p.wait("tensor", dve, tok_cb)
                for su in range(4):
                    tok_cr = p.op("tensor", lambda t, e=e, su=su: t.matmul(
                        self.ps[4][0:1, :], cb_tm[:, su * 8 + e: su * 8 + e + 1],
                        ish_sb[:, 384 - 128 * su: 896 - 128 * su],
                        start=(su == 0), stop=(su == 3)), inc=pe)
                p.wait("vector", pe, tok_cr)
                tok_ra = p.op("vector", lambda v: v.tensor_copy(
                    self.rowa[:], self.ps[4][0:1, :]), inc=dve)
                p.wait("tensor", dve, tok_ra)
                tok_bc = p.op("tensor", lambda t: t.matmul(
                    self.ps[5][:, :], self.onesrow_f[0:1, 0:128], self.rowa[:],
                    start=True, stop=True), inc=pe)
                p.wait("vector", pe, tok_bc)
                tok_cbe = p.op("vector", lambda v, e=e: v.tensor_copy(
                    cbc[e], self.ps[5][:, :]), inc=dve)
                p.wait("tensor", dve, tok_cbe)
            for ck in range(CK):
                tok_se = p.op("tensor", lambda t, ck=ck: t.matmul(
                    self.ps[4][0:1, :], seg_sb[:, ck:ck + 1], lnt(ck),
                    start=(ck == 0), stop=(ck == CK - 1)),
                    inc=pe if ck == CK - 1 else None)
            p.wait("scalar", pe, tok_se)
            tok_sg = p.op("scalar", lambda s: s.activation(
                self.rowa[:], self.ps[4][0:1, :], AF.Sigmoid), inc=act)
            p.wait("tensor", act, tok_sg)
            tok_bc = p.op("tensor", lambda t: t.matmul(
                self.ps[5][:, :], self.onesrow_f[0:1, 0:128], self.rowa[:],
                start=True, stop=True), inc=pe)
            p.wait("vector", pe, tok_bc)
            tok_sgb = p.op("vector", lambda v: v.tensor_copy(sgb, self.ps[5][:, :]),
                           inc=dve)
            p.wait("tensor", dve, tok_sgb)

        def do_tile(tt):
            tc = slice(tt * 512, (tt + 1) * 512)
            lnt = lambda ck, tt=tt: lnb[:, ck * TC + tt * 512:
                                        ck * TC + (tt + 1) * 512]
            for hk in range(CK):
                tok_hlast = gate_up(self.big1[:, hk * 512:(hk + 1) * 512],
                                    shgu_r[:, :, hk * 256:(hk + 1) * 256], hk,
                                    None, lnt)
            p.wait("tensor", dve, tok_hlast)
            for dc in range(CK):
                wp = dc % 2
                p.wait("sync", pe, wd_free[wp])
                tok_w = p.dma("sync",
                              wd_sl[wp].rearrange("p (hk d) -> p hk d", hk=8),
                              shd_r[:, 0:8, dc * 128:(dc + 1) * 128], qd6[wp])
                p.wait("sync", pe, gu_free[wp])
                tok_w2 = p.dma("sync",
                               gu_sl[wp][:, 0:1024].rearrange("p (hk d) -> p hk d", hk=8),
                               shd_r[:, 8:16, dc * 128:(dc + 1) * 128], qd6[wp])
                p.wait("tensor", qd6[wp], tok_w2)
                bank = 2 + dc % 2
                p.wait("tensor", dve, dbank_free[dc % 2])
                p.wait("tensor", act, gubA[bank])
                p.wait("tensor", dve, gubD[bank])
                for hk in range(8):
                    p.op("tensor", lambda t, hk=hk, wp=wp, bank=bank: t.matmul(
                        self.ps[bank][:, :], wd_sl[wp][:, hk * 128:(hk + 1) * 128],
                        self.big1[:, hk * 512:(hk + 1) * 512],
                        start=(hk == 0), stop=False), inc=None)
                for hk in range(8):
                    tok_d = p.op("tensor", lambda t, hk=hk, wp=wp, bank=bank: t.matmul(
                        self.ps[bank][:, :],
                        gu_sl[wp][:, hk * 128:(hk + 1) * 128],
                        self.big1[:, (8 + hk) * 512:(9 + hk) * 512],
                        start=False, stop=(hk == 7)), inc=pe)
                wd_free[wp] = tok_d
                gu_free[wp] = max(gu_free[wp], tok_d)
                xp = dc % 2
                p.wait("sync", dve, x_free[xp])
                tok_x = p.dma("sync", xstr[xp], src_r[:, dc, tc], q6x[xp])
                p.wait("vector", pe, tok_d)
                p.wait("vector", q6x[xp], tok_x)
                tok_sh = p.op("vector", lambda v, bank=bank: v.tensor_tensor(
                    self.tb[:], self.ps[bank][:, :], sgb, op=ALU.mult), inc=dve)
                dbank_free[dc % 2] = tok_sh
                gubD[bank] = tok_sh
                tok_o = p.op("vector", lambda v, xp=xp: v.tensor_tensor(
                    xtmp[:], self.tb[:], xstr[xp], op=ALU.add), inc=dve)
                x_free[xp] = tok_o
                p.wait("scalar", dve, tok_o)
                p.dma("scalar", dst_r[:, dc, tc], xtmp[:], qst)
                p.wait_all("vector", qst)
            for e in range(8):
                for hk in range(8):
                    tok_hlast = gate_up(
                        self.big1[:, (e * 8 + hk) * 512:(e * 8 + hk + 1) * 512],
                        wguC_r[e, :, :, hk * 256:(hk + 1) * 256],
                        e * 8 + hk, cbc[e], lnt)
            if tt + 1 < 4:
                comb_tile(tt + 1)
            p.wait("tensor", dve, tok_hlast)
            for dc in range(CK):
                bank = 2 + dc % 2
                p.wait("tensor", dve, dbank_free[dc % 2])
                p.wait("tensor", act, gubA[bank])
                p.wait("tensor", dve, gubD[bank])
                tok_d = 0
                for e in range(8):
                    wp = (dc * 8 + e) % 2
                    p.wait("sync", pe, wd_free[wp])
                    tok_w = p.dma("sync",
                                  wd_sl[wp].rearrange("p (hk d) -> p hk d", hk=8),
                                  wdC_r[e, :, :, dc * 128:(dc + 1) * 128], qd6[wp])
                    p.wait("tensor", qd6[wp], tok_w)
                    for hk in range(8):
                        tok_d = p.op("tensor", lambda t, e=e, hk=hk, wp=wp,
                                     bank=bank: t.matmul(
                            self.ps[bank][:, :], wd_sl[wp][:, hk * 128:(hk + 1) * 128],
                            self.big1[:, (e * 8 + hk) * 512:(e * 8 + hk + 1) * 512],
                            start=(e == 0 and hk == 0), stop=(e == 7 and hk == 7)),
                            inc=pe)
                    wd_free[wp] = tok_d
                xp = dc % 2
                p.wait("sync", dve, x_free[xp])
                tok_x = p.dma("sync", xstr[xp], dst_r[:, dc, tc], q6x[xp])
                p.wait("vector", pe, tok_d)
                p.wait("vector", q6x[xp], tok_x)
                tok_o = p.op("vector", lambda v, xp=xp, bank=bank: v.tensor_tensor(
                    xtmp[:], self.ps[bank][:, :], xstr[xp], op=ALU.add), inc=dve)
                x_free[xp] = tok_o
                dbank_free[dc % 2] = tok_o
                gubD[bank] = tok_o
                p.wait("scalar", dve, tok_o)
                p.dma("scalar", dst_r[:, dc, tc], xtmp[:], qst)
                p.wait_all("vector", qst)

        comb_tile(0)
        for tt in range(4):
            do_tile(tt)
        p.wait_all("sync", qst)
    def p7(self):
        p, nc = self.p, self.nc
        pe, dve, act, qld, qst, qgp = (self.pe, self.dve, self.act, self.qld,
                                       self.qst, self.qgp)
        lnb = self.big0
        lnt = lambda ck, tt: lnb[:, ck * TC + tt * 512: ck * TC + (tt + 1) * 512]
        w_sl = [self.lns[:, i * 2048:(i + 1) * 2048] for i in range(2)]
        kv_sl = [self.big1[:, i * 8192:(i + 1) * 8192] for i in range(2)]
        vk_sb = self.xs[:, 5440:7488]
        vkT_sb = self.xs[:, 2592:3648].bitcast(BF16)
        hbK = self.lns[:, 4096:4624]
        hbV = self.lns[:, 5152:5680]
        qhb = self.lns[:, 6144:8192]
        xstr = [self.xs[:, 3840 + i * 512: 3840 + (i + 1) * 512] for i in range(2)]
        xtmp = self.xs[:, 4864:5376]
        bpr = self.xs[:, 5376:5392]
        q7 = [p.sem("q7a"), p.sem("q7b")]
        q7k = [p.sem("q7ka"), p.sem("q7kb")]
        q7w = [p.sem("q7wa"), p.sem("q7wb")]
        q7m = p.sem("q7m")
        ccr = p.sem("ccr")
        x2_r = self.x2r.rearrange("(ck p) t -> p ck t", p=128)
        x3_r = self.x3r.rearrange("(ck p) t -> p ck t", p=128)
        wq_r = self.wqkvT.rearrange("(ck p) d -> p ck d", p=128)
        wp_r = self.wprojT.rearrange("(ck p) d -> p ck d", p=128)
        resq = {}
        qbp = p.sem("qbp")
        tok_bp = p.dma("sync", bpr, self.bprojR, qbp)

        wfree = [0, 0]
        for dc in range(CK):
            wp = dc % 2
            p.wait("gpsimd", pe, wfree[wp])
            tok_w = p.dma("gpsimd", w_sl[wp].rearrange("p (ck d) -> p ck d", ck=CK),
                          wq_r[:, :, dc * 128:(dc + 1) * 128], q7w[wp])
            p.wait("tensor", q7w[wp], tok_w)
            for tt in range(4):
                bank = tt % 2
                p.wait("tensor", act, resq.get(bank, 0))
                for ck in range(CK):
                    tok_m = p.op("tensor", lambda t, ck=ck, tt=tt, wp=wp, bank=bank: t.matmul(
                        self.ps[bank][:, :], w_sl[wp][:, ck * 128:(ck + 1) * 128],
                        lnt(ck, tt), start=(ck == 0), stop=(ck == CK - 1)),
                        inc=pe if ck == CK - 1 else None)
                p.wait("scalar", pe, tok_m)
                p.wait_all("scalar", qst)
                tok_e = p.op("scalar", lambda s, bank=bank: s.activation(
                    self.ta[:], self.ps[bank][:, :], AF.Relu), inc=act)
                resq[bank] = tok_e
                p.wait("scalar", act, tok_e)
                p.dma("scalar", self.qb_fm[dc * 128:(dc + 1) * 128,
                                           tt * 512:(tt + 1) * 512], self.ta[:], qst)
            wfree[wp] = pe.n

        kvfree = [0, 0]
        gi = 0
        for which in range(2):
            dstd = (self.kb_tm, self.vb_tm)[which]
            relu = (True, False)[which]
            base = 2048 + which * 2048
            for g in range(4):
                sp = gi % 2
                gi += 1
                p.wait("gpsimd", pe, kvfree[sp])
                tok_w = p.dma("gpsimd",
                              kv_sl[sp].rearrange("p (ck d) -> p ck d", ck=CK),
                              wq_r[:, :, base + g * 512: base + (g + 1) * 512],
                              q7k[sp])
                p.wait("tensor", q7k[sp], tok_w)
                for tsub in range(16):
                    bank = tsub % 2
                    p.wait("tensor", act, resq.get(bank, 0))
                    for ck in range(CK):
                        tok_m = p.op("tensor", lambda t, ck=ck, g=g, tsub=tsub,
                                     sp=sp, bank=bank: t.matmul(
                            self.ps[bank][:, :],
                            lnb[:, ck * TC + tsub * 128: ck * TC + (tsub + 1) * 128],
                            kv_sl[sp][:, ck * 512:(ck + 1) * 512],
                            start=(ck == 0), stop=(ck == CK - 1)),
                            inc=pe if ck == CK - 1 else None)
                    p.wait("scalar", pe, tok_m)
                    p.wait_all("scalar", qst)
                    tok_e = p.op("scalar", lambda s, bank=bank, relu=relu: s.activation(
                        self.ta[:], self.ps[bank][:, :],
                        AF.Relu if relu else AF.Copy), inc=act)
                    resq[bank] = tok_e
                    p.wait("scalar", act, tok_e)
                    p.dma("scalar", dstd[tsub * 128:(tsub + 1) * 128,
                                         g * 512:(g + 1) * 512], self.ta[:], qst)
                kvfree[sp] = pe.n
        p.wait_all("sync", qst)

        hb3K = hbK.rearrange("p (n e) -> p n e", e=33)
        hb3V = hbV.rearrange("p (n e) -> p n e", e=33)
        tok_on = p.op("vector", lambda v: v.memset(hb3V[:, :, 32:33], 1.0))
        p.wait("sync", dve, tok_on)
        hfree = 0
        for h in range(64):
            p.wait("sync", pe, hfree)
            p.dma("sync", hb3K[:, :, 0:32],
                  self.kb_tm[:, h * 32:(h + 1) * 32]
                  .rearrange("(n p) d -> p n d", p=128), q7m)
            tok_kv = p.dma("sync", hb3V[:, :, 0:32],
                           self.vb_tm[:, h * 32:(h + 1) * 32]
                           .rearrange("(n p) d -> p n d", p=128), q7m)
            p.wait("tensor", q7m, tok_kv)
            for n_ in range(16):
                tok_m = p.op("tensor", lambda t, n_=n_: t.matmul(
                    self.ps[4][0:33, 0:32], hb3V[:, n_, :], hb3K[:, n_, 0:32],
                    start=(n_ == 0), stop=(n_ == 15)), inc=pe)
            hfree = tok_m
            p.wait("vector", pe, tok_m)
            tok_v = p.op("vector", lambda v, h=h: v.tensor_copy(
                vk_sb.rearrange("a (h2 d) -> a h2 d", d=32)[0:33, h, :]
                if False else vk_sb[0:33, h * 32:(h + 1) * 32],
                self.ps[4][0:33, 0:32]), inc=dve)
            p.wait("tensor", dve, tok_v)
        p.wait("scalar", dve, dve.n)
        tok_st = p.dma("scalar", self.vkp[:, :], vk_sb[0:33, :], qst)
        p.wait_all("gpsimd", qst)
        tok_cc = p.op("gpsimd", lambda g: g.collective_compute(
            "AllReduce", ALU.add,
            replica_groups=[[0, 1], [2, 3], [4, 5], [6, 7]],
            ins=[self.vkp], outs=[self.vkr]), inc=ccr)
        p.wait("sync", ccr, tok_cc)
        q7v = p.sem("q7v")
        tok_ld = p.dma("sync", vk_sb[0:33, :], self.vkr[:, :], q7v)
        p.wait("vector", q7v, tok_ld)
        tok_cast = p.op("vector", lambda v: v.tensor_copy(
            qhb[0:33, 0:2048], vk_sb[0:33, :]), inc=dve)
        p.wait("tensor", dve, tok_cast)
        vkbf = qhb
        tfree = 0
        for h in range(64):
            tok_tr = p.op("tensor", lambda t, h=h: t.transpose(
                self.psT[0][0:32, 0:33], vkbf[0:33, h * 32:(h + 1) * 32],
                self.identb[0:33, 0:33]), inc=pe)
            p.wait("vector", pe, tok_tr)
            tok_ev = p.op("vector", lambda v, h=h: v.tensor_copy(
                vkT_sb[0:32, h * 33:(h + 1) * 33], self.psT[0][0:32, 0:33]), inc=dve)
            p.wait("tensor", dve, tok_ev)

        qfree = pe.n
        for h in range(64):
            p.wait("sync", pe, qfree)
            tok_q = p.dma("sync", qhb[0:32, :],
                          self.qb_fm[h * 32:(h + 1) * 32, :], q7m)
            p.wait("tensor", q7m, tok_q)
            for tt in range(4):
                p.wait("tensor", act, resq.get("onum", 0))
                tok_m = p.op("tensor", lambda t, h=h, tt=tt: t.matmul(
                    self.ps[4][0:33, :], vkT_sb[0:32, h * 33:(h + 1) * 33],
                    qhb[0:32, tt * 512:(tt + 1) * 512],
                    start=True, stop=True), inc=pe)
                qfree = tok_m
                p.wait("vector", pe, tok_m)
                tok_r = p.op("vector", lambda v: v.reciprocal(
                    self.rowa[:], self.ps[4][32:33, :]), inc=dve)
                p.wait("tensor", dve, tok_r)
                p.wait("tensor", dve, resq.get("odiv", 0))
                tok_b = p.op("tensor", lambda t: t.matmul(
                    self.ps[5][0:32, :], self.onesrow_f[0:1, 0:32], self.rowa[:],
                    start=True, stop=True), inc=pe)
                p.wait("scalar", pe, tok_b)
                tok_n = p.op("scalar", lambda s: s.activation(
                    self.ta[0:32, :], self.ps[4][0:32, :], AF.Copy), inc=act)
                resq["onum"] = tok_n
                p.wait("vector", act, tok_n)
                p.wait_all("vector", qst)
                tok_o = p.op("vector", lambda v: v.tensor_tensor(
                    self.tb[0:32, :], self.ta[0:32, :], self.ps[5][0:32, :],
                    op=ALU.mult), inc=dve)
                resq["odiv"] = tok_o
                p.wait("scalar", dve, tok_o)
                p.dma("scalar", self.abo[h * 32:(h + 1) * 32,
                                         tt * 512:(tt + 1) * 512],
                      self.tb[0:32, :], qst)
        p.wait_all("sync", qst)

        abo_res = self.big1[:, 0:32768]
        ab3 = abo_res.rearrange("p (ck t) -> p ck t", ck=CK)
        tok_ab = p.dma("sync", ab3,
                       self.abo.rearrange("(ck p) t -> p ck t", p=128), q7m)
        p.wait("tensor", q7m, tok_ab)
        p.wait("vector", qbp, tok_bp)
        xfree = [0, 0]
        bfree = [0, 0]
        for dc in range(CK):
            wp = dc % 2
            p.wait("gpsimd", pe, wfree[wp])
            tok_w = p.dma("gpsimd", w_sl[wp].rearrange("p (ck d) -> p ck d", ck=CK),
                          wp_r[:, :, dc * 128:(dc + 1) * 128], q7w[wp])
            p.wait("tensor", q7w[wp], tok_w)
            for tt in range(4):
                bank = tt % 2
                p.wait("tensor", dve, bfree[bank])
                p.wait("tensor", act, resq.get(bank, 0))
                for ck in range(CK):
                    tok_m = p.op("tensor", lambda t, ck=ck, tt=tt, wp=wp, bank=bank: t.matmul(
                        self.ps[bank][:, :], w_sl[wp][:, ck * 128:(ck + 1) * 128],
                        ab3[:, ck, tt * 512:(tt + 1) * 512],
                        start=(ck == 0), stop=(ck == CK - 1)),
                        inc=pe if ck == CK - 1 else None)
                xp = tt % 2
                p.wait("sync", dve, xfree[xp])
                tok_x = p.dma("sync", xstr[xp], x2_r[:, dc, tt * 512:(tt + 1) * 512],
                              q7[xp])
                p.wait("vector", pe, tok_m)
                p.wait("vector", q7[xp], tok_x)
                p.op("vector", lambda v, bank=bank, dc=dc: v.tensor_scalar(
                    xtmp[:], self.ps[bank][:, :], bpr[:, dc:dc + 1], None,
                    op0=ALU.add), inc=dve)
                tok_o = p.op("vector", lambda v, xp=xp: v.tensor_tensor(
                    xtmp[:], xtmp[:], xstr[xp], op=ALU.add), inc=dve)
                xfree[xp] = tok_o
                bfree[bank] = tok_o
                p.wait("scalar", dve, tok_o)
                p.dma("scalar", x3_r[:, dc, tt * 512:(tt + 1) * 512], xtmp[:], qst)
                p.wait_all("vector", qst)
            wfree[wp] = pe.n
        p.wait_all("sync", qst)
    def finish(self):
        p = self.p
        p.wait("gpsimd", self.qst, self.qst.n)
        p.wait("gpsimd", self.qld, self.qld.n)
        p.wait("gpsimd", self.qgp, self.qgp.n)


def build_nc(upto="all", dbg=()):
    nc = bass.Bass("TRN2", target_bir_lowering=False)
    with ExitStack() as ctx:
        k = K(nc, ctx, dbg)
        k.p0()
        k.p0b()
        if upto in ("p1", "p2", "p3", "p5", "p6", "p7", "all"):
            k.p1()
        if upto in ("p2", "p3", "p5", "p6", "p7", "all"):
            k.phase_sync("p1")
            k.cast_wait()
            k.p2()
        if upto in ("p3", "p5", "p6", "p7", "all"):
            k.phase_sync("p2")
            k.p4f()
            k.p3()
        if upto in ("p5", "p6", "p7", "all"):
            k.p4()
            k.phase_sync("p4")
            k.p5()
        if upto in ("p6", "p7", "all"):
            k.phase_sync("p5")
            k.ffn(k.x1r, k.x2r, "f1")
        if upto in ("p7", "all"):
            k.phase_sync("p6")
            k.ln_stream(k.x2r, k.big0,
                        dbg_out=k.ln_d if "ln_d" in k.dbg else None)
            k.p7()
        if upto == "all":
            k.phase_sync("p7")
            k.ln_stream(k.x3r, k.big0)
            k.phase_sync("ln4")
            k.ffn(k.x3r, k.out_fm, "f2")
        if upto == "all":
            pass
        k.finish()
        block = ctx.enter_context(nc.Block())
        k.p.emit(block)
    return nc


def kernel(**inputs):
    in_maps = stage(inputs)
    nc = build_nc()
    res = run_bass_kernel_spmd(nc, in_maps, core_ids=list(range(8)))
    x = np.asarray(inputs["L0_attn_a_hidden_states"])
    out = np.empty_like(x)
    for core in range(8):
        b_, par = core // 2, core % 2
        out[b_, par * TC:(par + 1) * TC, :] = res.results[core]["out_fm"].T
    return out
